# version 07
# baseline (speedup 1.0000x reference)
import contextlib
import numpy as np
import concourse.bass as bass
import concourse.mybir as mybir
from concourse.bass_utils import run_bass_kernel_spmd

F32 = mybir.dt.float32
BF16 = mybir.dt.bfloat16
I32 = mybir.dt.int32
AF = mybir.ActivationFunctionType
ALU = mybir.AluOpType
AX = mybir.AxisListType

P = 128
D = 2048
KC = 16
S = 8192
TOWN = 4096
H = 5461
HKC = 43
NW = H * D

T1 = 448
P1_TILES = [448] * 18 + [128]
T2 = 256
T3 = 512
T4A = 512
T4B = 256
SWEEPS = [(0, 8), (8, 8), (16, 8), (24, 8), (32, 8), (40, 3)]
HW_LAST = 85
W2A_KC = 22

TWO_PI = float(2 * np.pi)
PI_HI = float(np.float32(6.28125))
PI_LO = float(np.float32(TWO_PI - 6.28125))
LN_EPS = 1e-5
QLAD = [float(np.float32(10000.0 ** (-(2.0 ** b) / 512.0))) for b in range(9)]


def hw_of(hc):
    return HW_LAST if hc == HKC - 1 else 128


class Sem:
    def __init__(self, ctx, nc, name, step):
        self.h = ctx.enter_context(nc.semaphore(name))
        self.step = step
        self.n = 0

    def inc(self, instr):
        instr.then_inc(self.h, self.step)
        self.n += self.step
        return self.n


def build_program(phases=99, n1tiles=None):
    nc = bass.Bass("TRN2", target_bir_lowering=False)
    di = lambda name, shape, dt=F32: nc.dram_tensor(name, shape, dt, kind="ExternalInput").ap()
    xh = di("xh", [P, KC, S])
    xho = di("xho", [P, KC, TOWN])
    wqh = di("wqh", [P, KC, D]);  wkh = di("wkh", [P, KC, D])
    wvh = di("wvh", [P, KC, D]);  woh = di("woh", [P, KC, D])
    w1h = di("w1h", [P, KC, H]);  w3h = di("w3h", [P, KC, H])
    w2h = di("w2h", [P, HKC, D])
    bqkv = di("bqkv", [P, 4, KC])
    meta = di("meta", [P, 1])
    out_d = nc.dram_tensor("out", [P, KC, TOWN], F32, kind="ExternalOutput").ap()
    x1_d = nc.dram_tensor("x1_scr", [P, KC, TOWN], BF16, kind="Internal").ap()
    g_d = nc.dram_tensor("g_scr", [P, HKC, TOWN], BF16, kind="Internal").ap()
    u2a_d = nc.dram_tensor("u2a_scr", [P, KC, TOWN], F32, kind="Internal").ap()

    ctx = contextlib.ExitStack()
    sbg = lambda n, s, d=F32: ctx.enter_context(nc.sbuf_tensor(n, s, d))

    bias4 = sbg("bias4", [P, 4, KC])
    s0t = sbg("s0t", [P, 1])
    iota_t = sbg("iota_t", [P, 448])
    invf = sbg("invf", [P, 4])
    s0invf = sbg("s0invf", [P, 4])
    binv = sbg("binv", [P, 4])
    ji = sbg("ji", [P, 4], I32)
    jb = sbg("jb", [P, 4], I32)
    jf = sbg("jf", [P, 4])
    A = sbg("A", [P, KC])
    At = sbg("At", [P, KC])
    onesb = sbg("onesb", [P, 1], BF16)
    onesc = sbg("onesc", [P, 1])
    onesf = sbg("onesf", [1, P])
    halfpi = sbg("halfpi", [P, 1])
    scal = sbg("scal", [P, 12])
    sacc = sbg("sacc", [P, 3])
    acol = sbg("acol", [P, 1])
    wstr = [sbg("wstr0", [P, 512]), sbg("wstr1", [P, 512])]
    sA = sbg("sA", [1, 3])
    muinv = sbg("muinv", [1, 512])
    n2s = sbg("n2s", [1, 448])
    inv1 = sbg("inv1", [1, 448])

    sSET = Sem(ctx, nc, "sSET", 16)
    sAB = [Sem(ctx, nc, f"sAB{i}", 16) for i in range(2)]
    sWK = Sem(ctx, nc, "sWK", 16)
    sWV = Sem(ctx, nc, "sWV", 16)
    sWQ = Sem(ctx, nc, "sWQ", 16)
    sWO = Sem(ctx, nc, "sWO", 16)
    sXg = [Sem(ctx, nc, f"sXg{i}", 16) for i in range(2)]
    sXB = Sem(ctx, nc, "sXB", 16)
    sXR = Sem(ctx, nc, "sXR", 16)
    sTN = [Sem(ctx, nc, f"sTN{i}", 16) for i in range(2)]
    sG4 = [Sem(ctx, nc, f"sG4{i}", 16) for i in range(2)]
    sST = [Sem(ctx, nc, f"sST{i}", 16) for i in range(2)]
    sXL = [Sem(ctx, nc, f"sXL{i}", 16) for i in range(2)]
    sU = [Sem(ctx, nc, f"sU{i}", 16) for i in range(3)]
    sOUT = Sem(ctx, nc, "sOUT", 16)
    eT = Sem(ctx, nc, "eT", 1)
    eS = Sem(ctx, nc, "eS", 1)
    eV = Sem(ctx, nc, "eV", 1)
    eG = Sem(ctx, nc, "eG", 1)
    eP = Sem(ctx, nc, "eP", 1)

    block = ctx.enter_context(nc.Block())
    ALL_SEMS = [sSET, sWK, sWV, sWQ, sWO, sXB, sXR, sOUT, eT, eS, eV, eG] + \
        sAB + sXg + sTN + sG4 + sST + sXL + sU

    def barrier():
        @block.sync
        def _(sync):
            for sem in ALL_SEMS:
                sync.wait_ge(sem.h, sem.n)
            eP.inc(sync.nop())
        tgt = eP.n
        for name in ("tensor", "vector", "scalar", "gpsimd", "sync"):
            @getattr(block, name)
            def _(eng):
                eng.wait_ge(eP.h, tgt)

    @block.sync
    def _(sync):
        sSET.inc(sync.dma_start(out=bias4[:, :, :], in_=bqkv[:, :, :]))
        sSET.inc(sync.dma_start(out=s0t[:, :], in_=meta[:, :]))

    @block.gpsimd
    def _(g):
        g.iota(iota_t[:, :], [[1, 448]], channel_multiplier=0,
               allow_small_or_imprecise_dtypes=True)
        g.iota(ji[:, :], [[P, 4]], channel_multiplier=1,
               allow_small_or_imprecise_dtypes=True)
        g.memset(onesb[:, :], 1.0)
        g.memset(onesc[:, :], 1.0)
        g.memset(onesf[:, :], 1.0)
        g.memset(halfpi[:, :], float(np.pi / 2))
        g.memset(A[:, :], 0.0)
        g.memset(sacc[:, :], 0.0)
        g.memset(invf[:, :], 1.0)
        g.drain()
        eG.inc(g.nop())

    @block.vector
    def _(vec):
        vec.wait_ge(eG.h, eG.n)
        for b in range(9):
            vec.tensor_scalar(jb[:, :], ji[:, :], b, 1,
                              op0=ALU.logical_shift_right, op1=ALU.bitwise_and)
            vec.drain()
            vec.tensor_copy(jf[:, :], jb[:, :])
            vec.drain()
            vec.tensor_scalar(jf[:, :], jf[:, :], QLAD[b] - 1.0, 1.0,
                              op0=ALU.mult, op1=ALU.add)
            vec.drain()
            vec.tensor_mul(invf[:, :], invf[:, :], jf[:, :])
            vec.drain()
        vec.wait_ge(sSET.h, sSET.n)
        vec.tensor_scalar(s0invf[:, :], invf[:, :], s0t[:, 0:1], None, op0=ALU.mult)
        vec.drain()
        eV.inc(vec.nop())

    barrier()
    if phases < 1:
        @block.sync
        def _(sync):
            for sem in ALL_SEMS:
                sync.wait_ge(sem.h, sem.n)
        ctx.close()
        return nc

    absum_jobs = []
    for wsrc, col in ((w1h, 0), (w3h, 1), (w2h, 2)):
        for c in range(wsrc.shape[1]):
            for off in range(0, wsrc.shape[2], 512):
                absum_jobs.append((wsrc, c, off, min(512, wsrc.shape[2] - off), col))
    absM = {}
    absS = {"i": 0}

    def emit_absum(n):
        for _ in range(n):
            i = absS["i"]
            if i >= len(absum_jobs):
                return
            absS["i"] += 1
            wsrc, c, off, wdt, col = absum_jobs[i]
            q = i % 2

            @block.sync
            def _(sync, wsrc=wsrc, c=c, off=off, wdt=wdt, q=q, i=i):
                if i >= 2:
                    sync.wait_ge(eV.h, absM[(i - 2, "s")])
                absM[(i, "l")] = sAB[q].inc(sync.dma_start(out=wstr[q][:, 0:wdt],
                                                           in_=wsrc[:, c, off:off + wdt]))

            @block.vector
            def _(vec, wdt=wdt, q=q, col=col, i=i):
                vec.wait_ge(sAB[q].h, absM[(i, "l")])
                if i >= 1:
                    vec.wait_ge(eV.h, absM[(i - 1, "v")])
                ins = vec.reduce_sum(acol[:, 0:1], wstr[q][:, 0:wdt], axis=AX.X,
                                     apply_absolute_value=True)
                absM[(i, "s")] = eV.inc(ins)
                vec.wait_ge(eV.h, absM[(i, "s")])
                ins = vec.tensor_tensor(sacc[:, col:col + 1], sacc[:, col:col + 1],
                                        acol[:, 0:1], op=ALU.add)
                absM[(i, "v")] = eV.inc(ins)

    def emit_trig_base(base_imm, use_s0):
        @block.vector
        def _(vec):
            if use_s0:
                vec.scalar_tensor_tensor(binv[:, :], invf[:, :], float(base_imm),
                                         s0invf[:, :], op0=ALU.mult, op1=ALU.add)
            else:
                vec.tensor_scalar(binv[:, :], invf[:, :], float(base_imm), None,
                                  op0=ALU.mult)
            vec.drain()

    def emit_trig(tb, T, sinb, cosb, vmarks, smarks, chunks=(0, 1, 2, 3)):
        for c in chunks:
            @block.vector
            def _(vec, c=c):
                ang, kS, kC, rA, rB, ki = (tb["ang"], tb["kS"], tb["kC"],
                                           tb["rA"], tb["rB"], tb["ki"])
                if c > 0:
                    vec.wait_ge(eS.h, smarks[c - 1])
                vec.tensor_scalar(ang[:, 0:T], iota_t[:, 0:T], invf[:, c:c + 1],
                                  binv[:, c:c + 1], op0=ALU.mult, op1=ALU.add)
                vec.drain()
                vec.tensor_scalar(kS[:, 0:T], ang[:, 0:T], 1.0 / TWO_PI, None, op0=ALU.mult)
                vec.tensor_scalar(kC[:, 0:T], ang[:, 0:T], 1.0 / TWO_PI, 0.25,
                                  op0=ALU.mult, op1=ALU.add)
                vec.drain()
                vec.tensor_copy(ki[:, 0:T], kS[:, 0:T])
                vec.drain()
                vec.tensor_copy(kS[:, 0:T], ki[:, 0:T])
                vec.drain()
                vec.tensor_copy(ki[:, 0:T], kC[:, 0:T])
                vec.drain()
                vec.tensor_copy(kC[:, 0:T], ki[:, 0:T])
                vec.drain()
                vec.scalar_tensor_tensor(rA[:, 0:T], kS[:, 0:T], -PI_HI, ang[:, 0:T],
                                         op0=ALU.mult, op1=ALU.add)
                vec.scalar_tensor_tensor(rB[:, 0:T], kC[:, 0:T], -PI_HI, ang[:, 0:T],
                                         op0=ALU.mult, op1=ALU.add)
                vec.drain()
                vec.scalar_tensor_tensor(ang[:, 0:T], kC[:, 0:T], -PI_LO, rB[:, 0:T],
                                         op0=ALU.mult, op1=ALU.add)
                vec.drain()
                vec.scalar_tensor_tensor(rB[:, 0:T], kS[:, 0:T], -PI_LO, rA[:, 0:T],
                                         op0=ALU.mult, op1=ALU.add)
                vec.drain()
                vec.tensor_scalar(ang[:, 0:T], ang[:, 0:T], float(np.pi / 2), None,
                                  op0=ALU.add)
                vec.drain()
                vec.tensor_scalar(kC[:, 0:T], ang[:, 0:T], float(np.pi), None,
                                  op0=ALU.is_gt)
                vec.tensor_scalar(kS[:, 0:T], rB[:, 0:T], float(np.pi), None,
                                  op0=ALU.is_gt)
                vec.drain()
                vec.scalar_tensor_tensor(ang[:, 0:T], kC[:, 0:T], -TWO_PI, ang[:, 0:T],
                                         op0=ALU.mult, op1=ALU.add)
                vec.scalar_tensor_tensor(rB[:, 0:T], kS[:, 0:T], -TWO_PI, rB[:, 0:T],
                                         op0=ALU.mult, op1=ALU.add)
                vec.drain()
                vmarks[c] = eV.inc(vec.nop())

            @block.scalar
            def _(sc, c=c):
                sc.wait_ge(eV.h, vmarks[c])
                sc.activation(sinb[:, c, 0:T], tb["rB"][:, 0:T], AF.Sin)
                smarks[c] = eS.inc(sc.activation(cosb[:, c, 0:T], tb["ang"][:, 0:T],
                                                 AF.Sin))
        return eS.n

    p1 = contextlib.ExitStack()
    sb1 = lambda n, s, d=F32: p1.enter_context(nc.sbuf_tensor(n, s, d))
    wkA = sb1("wkA", [P, KC, D], BF16)
    wvA = sb1("wvA", [P, KC, D], BF16)
    xb = [sb1("xb0", [P, KC, T1], BF16), sb1("xb1", [P, KC, T1], BF16)]
    Kt = sb1("Kt", [P, KC, T1], BF16)
    k2b = [sb1(f"k2b{i}", [P, T1], BF16) for i in range(4)]
    sinb1 = sb1("sinb1", [P, 4, T1], BF16)
    cosb1 = sb1("cosb1", [P, 4, T1], BF16)
    tb1 = {k: sb1("t1" + k, [P, T1], I32 if k == "ki" else F32)
           for k in ("ang", "kS", "kC", "rA", "rB", "ki")}
    psK = [p1.enter_context(nc.psum_tensor(f"psK{i}", [P, T1], F32)) for i in range(2)]
    psV = [p1.enter_context(nc.psum_tensor(f"psV{i}", [P, T1], F32)) for i in range(4)]
    psn = p1.enter_context(nc.psum_tensor("psn", [1, T1], F32))
    psb = p1.enter_context(nc.psum_tensor("psb", [P, T1], F32))

    @block.gpsimd
    def _(g):
        for kc in range(KC):
            sWK.inc(g.dma_start(out=wkA[:, kc, :], in_=wkh[:, kc, :]))
        for kc in range(KC):
            sWV.inc(g.dma_start(out=wvA[:, kc, :], in_=wvh[:, kc, :]))

    s1 = {"xb": {}, "xb_free": {}, "kslot": {}, "vslot": {}, "kt_rd": {},
          "k2_rd": {}, "psn_rd": None, "psb_rd": None, "psb_mm": None,
          "evac": {}, "sq": {}, "vmm": {}, "n2mm": None, "n2sqrt": None,
          "n2inv": None, "trig": None}

    pos = 0
    p1_tiles = P1_TILES if n1tiles is None else P1_TILES[:n1tiles]
    for ti, T in enumerate(p1_tiles):
        base = pos
        pos += T
        q = ti % 2

        @block.gpsimd
        def _(g, ti=ti, q=q, base=base, T=T):
            if ti >= 2:
                g.wait_ge(eT.h, s1["xb_free"][q])
            sXg[q].inc(g.dma_start(out=xb[q][:, 0:8, 0:T], in_=xh[:, 0:8, base:base + T]))
            s1["xb"][ti] = sXg[q].inc(g.dma_start(out=xb[q][:, 8:16, 0:T],
                                                  in_=xh[:, 8:16, base:base + T]))

        emit_trig_base(base, False)
        tg1 = {"v": {}, "s": {}}

        def emit_n2(mc, ti=ti, T=T):
            @block.tensor
            def _(te, mc=mc, ti=ti, T=T):
                te.wait_ge(eS.h, s1["sq"][mc])
                if mc == 0 and s1["psn_rd"] is not None:
                    te.wait_ge(eS.h, s1["psn_rd"])
                mm = eT.inc(te.matmul(psn[0:1, 0:T], onesb[:, 0:1], k2b[mc % 4][:, 0:T],
                                      start=(mc == 0), stop=(mc == KC - 1),
                                      skip_group_check=True))
                s1["k2_rd"][mc % 4] = mm
                if mc == KC - 1:
                    s1["n2mm"] = mm

        for mc in range(KC):
            slot = mc % 2
            @block.tensor
            def _(te, ti=ti, mc=mc, slot=slot, q=q, T=T):
                if mc == 0:
                    te.wait_ge(sXg[q].h, s1["xb"][ti])
                    if ti == 0:
                        te.wait_ge(sWK.h, sWK.n)
                prev = s1["kslot"].get(slot)
                if prev is not None:
                    te.wait_ge(eV.h, prev)
                for kc in range(KC):
                    ins = te.matmul(psK[slot][:, 0:T], wkA[:, kc, mc * 128:(mc + 1) * 128],
                                    xb[q][:, kc, 0:T], start=(kc == 0), stop=(kc == KC - 1))
                s1["evac"][("mm", mc)] = eT.inc(ins)

            @block.vector
            def _(vec, ti=ti, mc=mc, slot=slot, T=T):
                vec.wait_ge(eT.h, s1["evac"][("mm", mc)])
                ins = vec.tensor_scalar(Kt[:, mc, 0:T], psK[slot][:, 0:T],
                                        bias4[:, 1, mc:mc + 1], None, op0=ALU.add)
                ev1 = eV.inc(ins)
                s1["kslot"][slot] = ev1
                s1["evac"][mc] = ev1

            @block.scalar
            def _(sc, ti=ti, mc=mc, T=T):
                sc.wait_ge(eV.h, s1["evac"][mc])
                prevn2 = s1["k2_rd"].get(mc % 4)
                if prevn2 is not None:
                    sc.wait_ge(eT.h, prevn2)
                s1["sq"][mc] = eS.inc(sc.activation(k2b[mc % 4][:, 0:T], Kt[:, mc, 0:T],
                                                    AF.Square))
            if mc >= 2:
                emit_n2(mc - 2)
            if mc in (2, 5, 8, 11):
                s1["trig"] = emit_trig(tb1, T, sinb1, cosb1, tg1["v"], tg1["s"],
                                       chunks=((mc - 2) // 3,))
        emit_n2(14)
        emit_n2(15)

        @block.scalar
        def _(sc, ti=ti, T=T):
            sc.wait_ge(eT.h, s1["n2mm"])
            mk = eS.inc(sc.activation(n2s[0:1, 0:T], psn[0:1, 0:T], AF.Sqrt))
            s1["psn_rd"] = mk
            s1["n2sqrt"] = mk

        @block.vector
        def _(vec, ti=ti, T=T):
            vec.wait_ge(eS.h, s1["n2sqrt"])
            vec.tensor_scalar(n2s[0:1, 0:T], n2s[0:1, 0:T], 1e-12, None, op0=ALU.max)
            vec.drain()
            vec.reciprocal(inv1[0:1, 0:T], n2s[0:1, 0:T])
            vec.drain()
            s1["n2inv"] = eV.inc(vec.nop())

        def emit_sttc(c, ti=ti, T=T):
            @block.vector
            def _(vec, c=c, ti=ti, T=T):
                if c == 0:
                    vec.wait_ge(eT.h, s1["psb_mm"])
                for par in (0, 1):
                    prevd = s1.get(("sttdump", par))
                    if prevd is not None:
                        vec.wait_ge(eV.h, prevd)
                if c >= 8:
                    vec.wait_ge(eS.h, s1["sq"][c])
                vec.tensor_tensor(Kt[:, c, 0:T], Kt[:, c, 0:T], psb[:, 0:T],
                                  op=ALU.mult)
                vec.drain()
                vec.wait_ge(eT.h, s1["vmm"][c])
                dump = tb1["rA"] if c % 2 == 0 else tb1["rB"]
                ins = vec.scalar_tensor_tensor(dump[:, 0:T], psV[c % 4][:, 0:T],
                                               bias4[:, 2, c:c + 1], Kt[:, c, 0:T],
                                               op0=ALU.add, op1=ALU.mult,
                                               accum_out=At[:, c:c + 1])
                mk = eV.inc(ins)
                s1[("sttdump", c % 2)] = mk
                s1["vslot"][c % 4] = mk
                s1["kt_rd"][c] = mk
                if c == KC - 1:
                    s1["psb_rd"] = mk

        for mc in range(KC):
            slot = mc % 4
            if mc == 4:
                @block.tensor
                def _(te, ti=ti, T=T):
                    te.wait_ge(eV.h, s1["n2inv"])
                    if s1["psb_rd"] is not None:
                        te.wait_ge(eV.h, s1["psb_rd"])
                    s1["psb_mm"] = eT.inc(te.matmul(psb[:, 0:T], onesf[0:1, :],
                                                    inv1[0:1, 0:T], start=True, stop=True,
                                                    skip_group_check=True))
            if mc >= 4:
                emit_sttc(mc - 4)
            @block.tensor
            def _(te, ti=ti, mc=mc, slot=slot, q=q, T=T):
                prev = s1["vslot"].get(slot)
                if prev is not None:
                    te.wait_ge(eV.h, prev)
                if ti == 0 and mc == 0:
                    te.wait_ge(sWV.h, sWV.n)
                for kc in range(KC):
                    ins = te.matmul(psV[slot][:, 0:T], wvA[:, kc, mc * 128:(mc + 1) * 128],
                                    xb[q][:, kc, 0:T], start=(kc == 0), stop=(kc == KC - 1))
                mm = eT.inc(ins)
                s1["vmm"][mc] = mm
                if mc == KC - 1:
                    s1["xb_free"][q] = mm

            if mc < 4:
                @block.vector
                def _(vec, ti=ti, mc=mc, T=T):
                    ang, kS, kC, rA = tb1["ang"], tb1["kS"], tb1["kC"], tb1["rA"]
                    c = mc
                    vec.wait_ge(eS.h, tg1["s"][c])
                    vec.wait_ge(eS.h, s1["sq"][c])
                    vec.wait_ge(eS.h, s1["sq"][c + 4])
                    for par in (0, 1):
                        prevd = s1.get(("sttdump", par))
                        if prevd is not None:
                            vec.wait_ge(eV.h, prevd)
                    vec.tensor_tensor(ang[:, 0:T], Kt[:, c, 0:T], cosb1[:, c, 0:T],
                                      op=ALU.mult)
                    vec.tensor_tensor(kS[:, 0:T], Kt[:, c + 4, 0:T], sinb1[:, c, 0:T],
                                      op=ALU.mult)
                    vec.tensor_tensor(kC[:, 0:T], Kt[:, c, 0:T], sinb1[:, c, 0:T],
                                      op=ALU.mult)
                    vec.tensor_tensor(rA[:, 0:T], Kt[:, c + 4, 0:T], cosb1[:, c, 0:T],
                                      op=ALU.mult)
                    vec.drain()
                    vec.tensor_sub(Kt[:, c, 0:T], ang[:, 0:T], kS[:, 0:T])
                    vec.tensor_add(Kt[:, c + 4, 0:T], rA[:, 0:T], kC[:, 0:T])
                    vec.drain()
                    eV.inc(vec.nop())

        for c in (12, 13, 14, 15):
            emit_sttc(c)

        @block.vector
        def _(vec, ti=ti, T=T):
            vec.drain()
            vec.tensor_add(A[:, :], A[:, :], At[:, :])
            vec.drain()
            eV.inc(vec.nop())

        emit_absum(13)

    barrier()
    p1.close()
    if phases < 2:
        @block.sync
        def _(sync):
            for sem in ALL_SEMS:
                sync.wait_ge(sem.h, sem.n)
        ctx.close()
        return nc

    p2 = contextlib.ExitStack()
    sb2 = lambda n, s, d=F32: p2.enter_context(nc.sbuf_tensor(n, s, d))
    wqA = sb2("wqA", [P, KC, D], BF16)
    woA = sb2("woA", [P, KC, D], BF16)
    xb2 = sb2("xb2", [P, KC, T2], BF16)
    xr2 = sb2("xr2", [P, KC, T2], BF16)
    Qt = sb2("Qt", [P, 8, T2], BF16)
    Ybf = sb2("Ybf", [P, KC, T2], BF16)
    u1 = sb2("u1", [P, KC, T2])
    u1b = [sb2(f"u1b{i}", [P, T2], BF16) for i in range(4)]
    sqb = [sb2(f"sqb{i}", [P, T2], BF16) for i in range(4)]
    x1full = sb2("x1full", [P, KC, T2], BF16)
    sinb2 = sb2("sinb2", [P, 4, T2], BF16)
    cosb2 = sb2("cosb2", [P, 4, T2], BF16)
    tb2 = {k: sb2("t2" + k, [P, T2], I32 if k == "ki" else F32)
           for k in ("ang", "kS", "kC", "rA", "rB", "ki")}
    psQ = [p2.enter_context(nc.psum_tensor(f"psQ{i}", [P, T2], F32)) for i in range(2)]
    psO = [p2.enter_context(nc.psum_tensor(f"psO{i}", [P, T2], F32)) for i in range(2)]
    psSx = p2.enter_context(nc.psum_tensor("psSx", [1, T2], F32))
    psSxx = p2.enter_context(nc.psum_tensor("psSxx", [1, T2], F32))
    psb2 = p2.enter_context(nc.psum_tensor("psb2", [P, 2 * T2], F32))

    s2 = {"xb": {}, "xr": {}, "qslot": {}, "oslot": {}, "qe": {}, "omm": {},
          "u1": {}, "u1rd": None, "u1rd_v": None, "statmm": None, "bc": None,
          "bcrd": None, "u1b_rd": {}, "x1st": {}, "x1v": {}, "sxrd": None,
          "xb_free": None, "xb_free2": {}, "xr_free": None, "ybf_free": None, "wo_last": None,
          "qt_rd": {}, "qt": {}, "ybf": {}, "sqb": {}, "trig": None,
          "var": None, "std": None, "muinv": None}

    @block.gpsimd
    def _(g):
        for kc in range(KC):
            sWQ.inc(g.dma_start(out=wqA[:, kc, :], in_=wqh[:, kc, :]))

    N2TILES = TOWN // T2

    def emit_x1(ti):
        base = ti * T2
        bc = s2[("bc", ti)]
        for mc in range(KC):
            @block.vector
            def _(vec, ti=ti, mc=mc, bc=bc):
                vec.wait_ge(eT.h, bc)
                if mc == 0:
                    vec.wait_ge(eS.h, s2["trig"])
                    prev = s2["x1st"].get(0)
                    if prev is not None:
                        vec.wait_ge(sST[0].h, prev)
                vec.tensor_tensor(tb2["ang"][:, :], u1[:, mc, :], psb2[:, 0:T2],
                                  op=ALU.subtract)
                vec.drain()
                ins = vec.tensor_tensor(x1full[:, mc, :], tb2["ang"][:, :],
                                        psb2[:, T2:2 * T2], op=ALU.mult)
                mk = eV.inc(ins)
                vec.drain()
                s2["x1v"][mc] = mk
                if mc == KC - 1:
                    s2["bcrd"] = mk
                    s2["u1rd_v"] = mk
                    s2["u1rd"] = eS.n

        @block.sync
        def _(sync, ti=ti, base=base):
            for mc in range(KC):
                sync.wait_ge(eV.h, s2["x1v"][mc])
            s2["x1st"][0] = sST[0].inc(
                sync.dma_start(out=x1_d[:, :, base:base + T2], in_=x1full[:, :, :]))

    for ti in range(N2TILES):
        base = ti * T2

        @block.gpsimd
        def _(g, ti=ti, base=base):
            if ti > 0:
                g.wait_ge(eT.h, s2["xb_free"])
            sXB.inc(g.dma_start(out=xb2[:, 0:8, :], in_=xho[:, 0:8, base:base + T2]))
            s2["xb"][ti] = sXB.inc(g.dma_start(out=xb2[:, 8:16, :],
                                               in_=xho[:, 8:16, base:base + T2]))
            if ti > 0:
                g.wait_ge(eV.h, s2["xr_free"])
            sXR.inc(g.dma_start(out=xr2[:, 0:8, :], in_=xho[:, 0:8, base:base + T2]))
            s2["xr"][ti] = sXR.inc(g.dma_start(out=xr2[:, 8:16, :],
                                               in_=xho[:, 8:16, base:base + T2]))

        if ti == 0:
            @block.gpsimd
            def _(g):
                for kc in range(KC):
                    sWO.inc(g.dma_start(out=woA[:, kc, :], in_=woh[:, kc, :]))

        emit_trig_base(base, True)
        tg2 = {"v": {}, "s": {}}

        for mc in range(KC):
            slot = mc % 2
            @block.tensor
            def _(te, ti=ti, mc=mc, slot=slot):
                if mc == 0:
                    te.wait_ge(sXB.h, s2["xb"][ti])
                    if ti == 0:
                        te.wait_ge(sWQ.h, sWQ.n)
                prev = s2["qslot"].get(slot)
                if prev is not None:
                    eng, mk = prev
                    te.wait_ge(eng.h, mk)
                for kc in range(KC):
                    ins = te.matmul(psQ[slot][:, :], wqA[:, kc, mc * 128:(mc + 1) * 128],
                                    xb2[:, kc, :], start=(kc == 0), stop=(kc == KC - 1))
                mm = eT.inc(ins)
                s2["qe"][mc] = mm
                if mc == KC - 1:
                    s2["xb_free"] = mm

            if mc < 8:
                @block.vector
                def _(vec, ti=ti, mc=mc, slot=slot):
                    vec.wait_ge(eT.h, s2["qe"][mc])
                    ins = vec.tensor_scalar(Qt[:, mc, :], psQ[slot][:, :],
                                            bias4[:, 0, mc:mc + 1], None, op0=ALU.add)
                    mk = eV.inc(ins)
                    vec.drain()
                    s2["qslot"][slot] = (eV, mk)
                    s2["qt"][mc] = mk
            else:
                @block.vector
                def _(vec, ti=ti, mc=mc, slot=slot):
                    vec.wait_ge(eT.h, s2["qe"][mc])
                    if ti > 0 and mc == 8:
                        vec.wait_ge(eT.h, s2["ybf_free"])
                    ins = vec.tensor_scalar(Ybf[:, mc, :], psQ[slot][:, :],
                                            bias4[:, 0, mc:mc + 1], A[:, mc:mc + 1],
                                            op0=ALU.add, op1=ALU.mult)
                    mk = eV.inc(ins)
                    s2["qslot"][slot] = (eV, mk)
                    s2["ybf"][mc] = (eV, mk)
            if mc % 4 == 3:
                s2["trig"] = emit_trig(tb2, T2, sinb2, cosb2, tg2["v"], tg2["s"],
                                       chunks=(mc // 4,))

        if ti > 0:
            emit_x1(ti - 1)

        @block.vector
        def _(vec, ti=ti):
            ang, kS, kC, rA = tb2["ang"], tb2["kS"], tb2["kC"], tb2["rA"]
            vec.wait_ge(eS.h, s2["trig"])
            if ti > 0:
                vec.wait_ge(eT.h, s2["ybf_free"])
            for c in range(4):
                vec.wait_ge(eV.h, s2["qt"][c])
                vec.wait_ge(eV.h, s2["qt"][c + 4])
                vec.tensor_tensor(ang[:, :], Qt[:, c, :], cosb2[:, c, :], op=ALU.mult)
                vec.tensor_tensor(kS[:, :], Qt[:, c + 4, :], sinb2[:, c, :], op=ALU.mult)
                vec.tensor_tensor(kC[:, :], Qt[:, c, :], sinb2[:, c, :], op=ALU.mult)
                vec.tensor_tensor(rA[:, :], Qt[:, c + 4, :], cosb2[:, c, :], op=ALU.mult)
                vec.drain()
                vec.tensor_sub(ang[:, :], ang[:, :], kS[:, :])
                vec.tensor_add(rA[:, :], rA[:, :], kC[:, :])
                vec.drain()
                vec.tensor_scalar(Ybf[:, c, :], ang[:, :], A[:, c:c + 1], None, op0=ALU.mult)
                vec.tensor_scalar(Ybf[:, c + 4, :], rA[:, :], A[:, c + 4:c + 5], None,
                                  op0=ALU.mult)
                vec.drain()
                mk = eV.inc(vec.nop())
                s2["qt_rd"][c] = mk
                s2["qt_rd"][c + 4] = mk
                s2["ybf"][c] = (eV, mk)
                s2["ybf"][c + 4] = (eV, mk)

        def emit_stats2(mc, ti=ti):
            @block.tensor
            def _(te, ti=ti, mc=mc):
                kq = mc % 4
                te.wait_ge(eV.h, s2["sqb"][mc])
                te.wait_ge(eS.h, s2[("u1c", mc)])
                if mc == 0 and s2["sxrd"] is not None:
                    te.wait_ge(eV.h, s2["sxrd"])
                te.matmul(psSx[0:1, :], onesb[:, 0:1], u1b[kq][:, :],
                          start=(mc == 0), stop=(mc == KC - 1), skip_group_check=True)
                m2 = te.matmul(psSxx[0:1, :], onesb[:, 0:1], sqb[kq][:, :],
                               start=(mc == 0), stop=(mc == KC - 1), skip_group_check=True)
                mm = eT.inc(m2)
                s2["u1b_rd"][kq] = mm
                if mc == KC - 1:
                    s2["statmm"] = mm

        for mc in range(KC):
            slot = mc % 2
            @block.tensor
            def _(te, ti=ti, mc=mc, slot=slot):
                if mc == 0:
                    for c in range(KC):
                        eng, mk = s2["ybf"][c]
                        te.wait_ge(eng.h, mk)
                    if ti == 0:
                        te.wait_ge(sWO.h, sWO.n)
                prev = s2["oslot"].get(slot)
                if prev is not None:
                    te.wait_ge(eV.h, prev)
                for kc in range(KC):
                    ins = te.matmul(psO[slot][:, :], woA[:, kc, mc * 128:(mc + 1) * 128],
                                    Ybf[:, kc, :], start=(kc == 0), stop=(kc == KC - 1))
                mm = eT.inc(ins)
                s2["omm"][mc] = mm
                if mc == KC - 1:
                    s2["ybf_free"] = mm

            @block.vector
            def _(vec, ti=ti, mc=mc, slot=slot):
                vec.wait_ge(eT.h, s2["omm"][mc])
                if mc == 0:
                    vec.wait_ge(sXR.h, s2["xr"][ti])
                    if s2["u1rd"] is not None:
                        vec.wait_ge(eS.h, s2["u1rd"])
                        vec.wait_ge(eV.h, s2["u1rd_v"])
                vec.scalar_tensor_tensor(u1[:, mc, :], xr2[:, mc, :], 2.0, psO[slot][:, :],
                                         op0=ALU.mult, op1=ALU.add)
                vec.drain()
                ins = vec.tensor_scalar(u1[:, mc, :], u1[:, mc, :], bias4[:, 3, mc:mc + 1],
                                        None, op0=ALU.add)
                mk = eV.inc(ins)
                s2["oslot"][slot] = mk
                s2["u1"][mc] = mk
                if mc == KC - 1:
                    s2["xr_free"] = mk

            @block.scalar
            def _(sc, ti=ti, mc=mc):
                kq = mc % 4
                sc.wait_ge(eV.h, s2["u1"][mc])
                prev = s2["u1b_rd"].get(kq)
                if prev is not None:
                    sc.wait_ge(eT.h, prev)
                s2[("u1c", mc)] = eS.inc(sc.copy(u1b[kq][:, :], u1[:, mc, :]))

            @block.vector
            def _(vec, ti=ti, mc=mc):
                kq = mc % 4
                vec.wait_ge(eV.h, s2["u1"][mc])
                prev = s2["u1b_rd"].get(kq)
                if prev is not None:
                    vec.wait_ge(eT.h, prev)
                ins = vec.tensor_mul(sqb[kq][:, :], u1[:, mc, :], u1[:, mc, :])
                s2["sqb"][mc] = eV.inc(ins)

            if mc >= 2:
                emit_stats2(mc - 2)
        emit_stats2(14)
        emit_stats2(15)

        @block.vector
        def _(vec, ti=ti):
            vec.wait_ge(eT.h, s2["statmm"])
            vec.tensor_scalar(muinv[0:1, 0:T2], psSx[0:1, :], 1.0 / D, None, op0=ALU.mult)
            vec.tensor_scalar(n2s[0:1, 0:T2], psSxx[0:1, :], 1.0 / D, None, op0=ALU.mult)
            vec.drain()
            s2["sxrd"] = eV.inc(vec.nop())
            vec.tensor_tensor(inv1[0:1, 0:T2], muinv[0:1, 0:T2], muinv[0:1, 0:T2],
                              op=ALU.mult)
            vec.drain()
            vec.tensor_sub(n2s[0:1, 0:T2], n2s[0:1, 0:T2], inv1[0:1, 0:T2])
            vec.drain()
            ins = vec.tensor_scalar(n2s[0:1, 0:T2], n2s[0:1, 0:T2], LN_EPS, None, op0=ALU.add)
            s2["var"] = eV.inc(ins)

        @block.scalar
        def _(sc, ti=ti):
            sc.wait_ge(eV.h, s2["var"])
            s2["std"] = eS.inc(sc.activation(n2s[0:1, 0:T2], n2s[0:1, 0:T2], AF.Sqrt))

        @block.vector
        def _(vec, ti=ti):
            vec.wait_ge(eS.h, s2["std"])
            vec.reciprocal(muinv[0:1, T2:2 * T2], n2s[0:1, 0:T2])
            vec.drain()
            s2["muinv"] = eV.inc(vec.nop())

        @block.tensor
        def _(te, ti=ti):
            te.wait_ge(eV.h, s2["muinv"])
            if s2["bcrd"] is not None:
                te.wait_ge(eV.h, s2["bcrd"])
            s2[("bc", ti)] = eT.inc(te.matmul(psb2[:, :], onesf[0:1, :], muinv[0:1, :],
                                              start=True, stop=True,
                                              skip_group_check=True))

        emit_absum(18)

    emit_x1(N2TILES - 1)

    emit_absum(len(absum_jobs))

    @block.tensor
    def _(te):
        te.wait_ge(eV.h, eV.n)
        te.wait_ge(eS.h, eS.n)
        eT.inc(te.matmul(psSx[0:1, 0:3], onesc[:, 0:1], sacc[:, :], start=True, stop=True,
                         skip_group_check=True))

    @block.vector
    def _(vec):
        vec.wait_ge(eT.h, eT.n)
        vec.tensor_scalar(sA[0:1, :], psSx[0:1, 0:3], 1.0 / float(NW), None, op0=ALU.mult)
        vec.drain()
        eV.inc(vec.nop())

    @block.tensor
    def _(te):
        te.wait_ge(eV.h, eV.n)
        eT.inc(te.matmul(psb2[:, 0:3], onesf[0:1, :], sA[0:1, :], start=True, stop=True,
                         skip_group_check=True))

    @block.vector
    def _(vec):
        vec.wait_ge(eT.h, eT.n)
        vec.tensor_copy(scal[:, 0:3], psb2[:, 0:3])
        vec.drain()
        vec.tensor_scalar(scal[:, 3:6], scal[:, 0:3], 0.5, None, op0=ALU.mult)
        vec.tensor_scalar(scal[:, 6:9], scal[:, 0:3], -0.5, None, op0=ALU.mult)
        vec.drain()
        eV.inc(vec.nop())

    barrier()
    p2.close()
    if phases < 3:
        @block.sync
        def _(sync):
            for sem in ALL_SEMS:
                sync.wait_ge(sem.h, sem.n)
        ctx.close()
        return nc

    p3 = contextlib.ExitStack()
    sb3 = lambda n, s, d=F32: p3.enter_context(nc.sbuf_tensor(n, s, d))
    t1A = [sb3("t1A0", [P, KC, 1024], BF16), sb3("t1A1", [P, KC, 1024], BF16)]
    t3A = [sb3("t3A0", [P, KC, 1024], BF16), sb3("t3A1", [P, KC, 1024], BF16)]
    x1t = [sb3("x1t0", [P, KC, T3], BF16), sb3("x1t1", [P, KC, T3], BF16)]
    gt = [sb3("gt0", [P, 8, T3], BF16), sb3("gt1", [P, 8, T3], BF16)]
    silb = sb3("silb", [P, T3])
    wst3 = [sb3("wst30", [P, 1024]), sb3("wst31", [P, 1024])]
    psH1 = [p3.enter_context(nc.psum_tensor(f"psH1{i}", [P, T3], F32)) for i in range(2)]
    psH3 = [p3.enter_context(nc.psum_tensor(f"psH3{i}", [P, T3], F32)) for i in range(2)]

    s3 = {"x1": {}, "x1_free": {}, "h1slot": {}, "h3slot": {}, "g_rd": {},
          "arena_tern": {}, "arena_free": {}, "strip": {}, "strip_g": {},
          "sil_rd": None, "h1": {}, "h3": {}, "gv": {}}

    def emit_tern_arena(j, lo=0, hi=32):
        hc0, nhc = SWEEPS[j]
        hoff = hc0 * 128
        hlen = (H - hoff) if j == len(SWEEPS) - 1 else nhc * 128
        par = j % 2
        jobs = [(w1h, t1A[par], 3, 6, kc) for kc in range(KC)] + \
               [(w3h, t3A[par], 4, 7, kc) for kc in range(KC)]
        for idx, (wsrc, dstA, colp, coln, kc) in enumerate(jobs):
            if not (lo <= idx < hi):
                continue
            q = idx % 2
            @block.gpsimd
            def _(g, wsrc=wsrc, kc=kc, q=q, idx=idx, j=j, par=par,
                  hoff=hoff, hlen=hlen):
                if idx == 0 and j >= 2:
                    g.wait_ge(eT.h, s3["arena_free"][par])
                prevs = s3["strip_g"].get(q)
                if prevs is not None:
                    g.wait_ge(eV.h, prevs)
                sTN[q].inc(g.dma_start(out=wst3[q][:, 0:hlen],
                                       in_=wsrc[:, kc, hoff:hoff + hlen]))

            @block.vector
            def _(vec, dstA=dstA, colp=colp, coln=coln, kc=kc, q=q,
                  idx=idx, j=j, hlen=hlen, njobs=len(jobs)):
                vec.wait_ge(sTN[q].h, sTN[q].n)
                vec.tensor_scalar(dstA[:, kc, 0:hlen], wst3[q][:, 0:hlen],
                                  scal[:, colp:colp + 1], None, op0=ALU.is_gt)
                vec.drain()
                vec.tensor_scalar(wst3[q][:, 0:hlen], wst3[q][:, 0:hlen],
                                  scal[:, coln:coln + 1], None, op0=ALU.is_lt)
                vec.drain()
                ins = vec.tensor_tensor(dstA[:, kc, 0:hlen], dstA[:, kc, 0:hlen],
                                        wst3[q][:, 0:hlen], op=ALU.subtract)
                mk = eV.inc(ins)
                vec.drain()
                s3["strip"][(j, idx)] = mk
                s3["strip_g"][q] = mk
                if idx == njobs - 1:
                    s3["arena_tern"][j] = mk

    emit_tern_arena(0)

    N3TILES = TOWN // T3
    gtile = 0
    for j, (hc0, nhc) in enumerate(SWEEPS):
        par = j % 2
        for ti in range(N3TILES):
            base = ti * T3
            q = gtile % 2
            if j + 1 < len(SWEEPS):
                emit_tern_arena(j + 1, lo=4 * ti, hi=4 * ti + 4)

            @block.sync
            def _(sync, base=base, q=q, gtile=gtile):
                prev = s3["x1_free"].get(q)
                if prev is not None:
                    sync.wait_ge(eT.h, prev)
                s3["x1"][gtile] = sXL[q].inc(sync.dma_start(out=x1t[q][:, :, :],
                                                            in_=x1_d[:, :, base:base + T3]))

            for i in range(nhc):
                hw = hw_of(hc0 + i)
                hq = i % 2
                @block.tensor
                def _(te, j=j, ti=ti, i=i, hw=hw, hq=hq, q=q, par=par, gtile=gtile):
                    if i == 0:
                        te.wait_ge(sXL[q].h, s3["x1"][gtile])
                        if ti == 0:
                            te.wait_ge(eV.h, s3["arena_tern"][j])
                    prev = s3["h1slot"].get(hq)
                    if prev is not None:
                        te.wait_ge(eS.h, prev)
                    for kc in range(KC):
                        ins = te.matmul(psH1[hq][0:hw, :],
                                        t1A[par][:, kc, i * 128:i * 128 + hw],
                                        x1t[q][:, kc, :], start=(kc == 0), stop=(kc == KC - 1))
                    s3["h1"][i] = eT.inc(ins)
                    prev = s3["h3slot"].get(hq)
                    if prev is not None:
                        te.wait_ge(eV.h, prev)
                    for kc in range(KC):
                        ins = te.matmul(psH3[hq][0:hw, :],
                                        t3A[par][:, kc, i * 128:i * 128 + hw],
                                        x1t[q][:, kc, :], start=(kc == 0), stop=(kc == KC - 1))
                    mm = eT.inc(ins)
                    s3["h3"][i] = mm
                    if i == nhc - 1:
                        s3["x1_free"][q] = mm
                        if ti == N3TILES - 1:
                            s3["arena_free"][par] = mm

                @block.scalar
                def _(sc, i=i, hw=hw, hq=hq):
                    sc.wait_ge(eT.h, s3["h1"][i])
                    if s3["sil_rd"] is not None:
                        sc.wait_ge(eV.h, s3["sil_rd"])
                    mk = eS.inc(sc.activation(silb[0:hw, :], psH1[hq][0:hw, :], AF.Silu,
                                              scale=scal[0:hw, 0:1]))
                    s3["h1slot"][hq] = mk
                    s3["sil"] = mk

                @block.vector
                def _(vec, i=i, hw=hw, hq=hq, q=q):
                    vec.wait_ge(eS.h, s3["sil"])
                    vec.wait_ge(eT.h, s3["h3"][i])
                    prev = s3["g_rd"].get((q, i))
                    if prev is not None:
                        vec.wait_ge(sST[q].h, prev)
                    ins = vec.scalar_tensor_tensor(gt[q][0:hw, i, :], psH3[hq][0:hw, :],
                                                   scal[0:hw, 1:2], silb[0:hw, :],
                                                   op0=ALU.mult, op1=ALU.mult)
                    mk = eV.inc(ins)
                    s3["h3slot"][hq] = mk
                    s3["gv"][i] = mk
                    s3["sil_rd"] = mk

            @block.sync
            def _(sync, base=base, q=q, nhc=nhc, hc0=hc0):
                for i in range(nhc):
                    sync.wait_ge(eV.h, s3["gv"][i])
                mk = sST[q].inc(sync.dma_start(out=g_d[:, hc0:hc0 + nhc, base:base + T3],
                                               in_=gt[q][:, 0:nhc, :]))
                for i in range(nhc):
                    s3["g_rd"][(q, i)] = mk
            gtile += 1

    barrier()
    p3.close()
    if phases < 4:
        @block.sync
        def _(sync):
            for sem in ALL_SEMS:
                sync.wait_ge(sem.h, sem.n)
        ctx.close()
        return nc

    p4a = contextlib.ExitStack()
    sb4a = lambda n, s, d=F32: p4a.enter_context(nc.sbuf_tensor(n, s, d))
    t2A = sb4a("t2A", [P, W2A_KC, D], BF16)
    gta = [sb4a("gta0", [P, W2A_KC, T4A], BF16), sb4a("gta1", [P, W2A_KC, T4A], BF16)]
    u2s = [sb4a("u2s0", [P, T4A]), sb4a("u2s1", [P, T4A]), sb4a("u2s2", [P, T4A])]
    wst4 = [sb4a("wst40", [P, D]), sb4a("wst41", [P, D])]
    psU = [p4a.enter_context(nc.psum_tensor(f"psU{i}", [P, T4A], F32)) for i in range(3)]

    s4 = {"g": {}, "g_free": {}, "uslot": {}, "u2st": {}, "tern": {}, "strip4": {},
          "strip_g": {}, "umm": {}, "ue": {}}

    def emit_tern_w2(kc_list, dstA, tag):
        for idx, kc in enumerate(kc_list):
            q = idx % 2
            @block.gpsimd
            def _(g, kc=kc, q=q, idx=idx, tag=tag):
                prevs = s4["strip_g"].get((tag, q))
                if prevs is not None:
                    g.wait_ge(eV.h, prevs)
                sTN[q].inc(g.dma_start(out=wst4[q][:, :] if tag == "a" else wst4b[q][:, :],
                                       in_=w2h[:, kc, :]))

            @block.vector
            def _(vec, kc=kc, q=q, idx=idx, dstA=dstA, tag=tag, nj=len(kc_list)):
                vec.wait_ge(sTN[q].h, sTN[q].n)
                src = wst4[q] if tag == "a" else wst4b[q]
                vec.tensor_scalar(dstA[:, idx, :], src[:, :], scal[:, 5:6], None,
                                  op0=ALU.is_gt)
                vec.drain()
                vec.tensor_scalar(src[:, :], src[:, :], scal[:, 8:9], None, op0=ALU.is_lt)
                vec.drain()
                ins = vec.tensor_tensor(dstA[:, idx, :], dstA[:, idx, :], src[:, :],
                                        op=ALU.subtract)
                mk = eV.inc(ins)
                vec.drain()
                s4["strip4"][(tag, idx)] = mk
                s4["strip_g"][(tag, q)] = mk
                if idx == nj - 1:
                    s4["tern"][tag] = mk

    emit_tern_w2(list(range(W2A_KC)), t2A, "a")

    N4ATILES = TOWN // T4A
    for ti in range(N4ATILES):
        base = ti * T4A
        q = ti % 2

        @block.sync
        def _(sync, ti=ti, base=base, q=q):
            prev = s4["g_free"].get(q)
            if prev is not None:
                sync.wait_ge(eT.h, prev)
            sXL[q].inc(sync.dma_start(out=gta[q][:, 0:11, :],
                                      in_=g_d[:, 0:11, base:base + T4A]))
            s4["g"][ti] = sXL[q].inc(sync.dma_start(out=gta[q][:, 11:W2A_KC, :],
                                                    in_=g_d[:, 11:W2A_KC, base:base + T4A]))

        for mc in range(KC):
            slot = mc % 3
            @block.tensor
            def _(te, ti=ti, mc=mc, slot=slot, q=q):
                if mc == 0:
                    te.wait_ge(sXL[q].h, s4["g"][ti])
                    if ti == 0:
                        te.wait_ge(eV.h, s4["tern"]["a"])
                prev = s4["uslot"].get(slot)
                if prev is not None:
                    te.wait_ge(eS.h, prev)
                for kc in range(W2A_KC):
                    ins = te.matmul(psU[slot][:, :], t2A[:, kc, mc * 128:(mc + 1) * 128],
                                    gta[q][:, kc, :], start=(kc == 0), stop=(kc == W2A_KC - 1))
                mm = eT.inc(ins)
                s4["umm"][mc] = mm
                if mc == KC - 1:
                    s4["g_free"][q] = mm

            @block.scalar
            def _(sc, ti=ti, mc=mc, slot=slot, base=base):
                sc.wait_ge(eT.h, s4["umm"][mc])
                prev = s4["u2st"].get(slot)
                if prev is not None:
                    sc.wait_ge(sU[slot].h, prev)
                eS.inc(sc.activation(u2s[slot][:, :], psU[slot][:, :], AF.Copy,
                                     scale=scal[:, 2:3]))
                s4["uslot"][slot] = eS.n
                sc.wait_ge(eS.h, eS.n)
                mk = sU[slot].inc(sc.dma_start(out=u2a_d[:, mc, base:base + T4A],
                                               in_=u2s[slot][:, :]))
                s4["u2st"][slot] = mk

    barrier()
    p4a.close()
    if phases < 5:
        @block.sync
        def _(sync):
            for sem in ALL_SEMS:
                sync.wait_ge(sem.h, sem.n)
        ctx.close()
        return nc

    W2B0 = W2A_KC
    W2B_KCS = list(range(W2B0, HKC))
    NB = len(W2B_KCS)
    p4b = contextlib.ExitStack()
    sb4b = lambda n, s, d=F32: p4b.enter_context(nc.sbuf_tensor(n, s, d))
    t2B = sb4b("t2B", [P, NB, D], BF16)
    gtb = [sb4b("gtb0", [P, NB, T4B], BF16), sb4b("gtb1", [P, NB, T4B], BF16)]
    u2a_t = sb4b("u2a_t", [P, KC, T4B])
    x1bt = sb4b("x1bt", [P, KC, T4B], BF16)
    u2f = sb4b("u2f", [P, KC, T4B])
    u2bb = [sb4b(f"u2bb{i}", [P, T4B], BF16) for i in range(4)]
    sq4 = [sb4b(f"sq4{i}", [P, T4B], BF16) for i in range(4)]
    outt = sb4b("outt", [P, KC, T4B])
    scr4 = sb4b("scr4", [P, T4B])
    wst4b = [sb4b("wst4b0", [P, D]), sb4b("wst4b1", [P, D])]
    psU2 = [p4b.enter_context(nc.psum_tensor(f"psU2{i}", [P, T4B], F32)) for i in range(2)]
    psSx4 = p4b.enter_context(nc.psum_tensor("psSx4", [1, T4B], F32))
    psSxx4 = p4b.enter_context(nc.psum_tensor("psSxx4", [1, T4B], F32))
    psb4 = p4b.enter_context(nc.psum_tensor("psb4", [P, 2 * T4B], F32))

    emit_tern_w2(W2B_KCS, t2B, "b")

    sb_ = {"g": {}, "g_free": {}, "ua": {}, "ua_free": None, "uslot": {}, "u2": {},
           "u2rd": None, "u2rd_v": None, "u2b_rd": {}, "statmm": None, "bc": None,
           "bcrd": None, "outv": {}, "outst": {}, "sxrd": None, "umm": {},
           "sqb": {}, "var": None, "std": None, "muinv": None}
    N4BTILES = TOWN // T4B
    for ti in range(N4BTILES):
        base = ti * T4B
        q = ti % 2

        @block.gpsimd
        def _(g, ti=ti, base=base, q=q):
            prev = sb_["g_free"].get(q)
            if prev is not None:
                g.wait_ge(eT.h, prev)
            sG4[q].inc(g.dma_start(out=gtb[q][:, 0:11, :],
                                   in_=g_d[:, W2B0:W2B0 + 11, base:base + T4B]))
            sb_["g"][ti] = sG4[q].inc(g.dma_start(out=gtb[q][:, 11:NB, :],
                                                  in_=g_d[:, W2B0 + 11:HKC, base:base + T4B]))

        @block.sync
        def _(sync, ti=ti, base=base):
            if sb_["ua_free"] is not None:
                sync.wait_ge(eV.h, sb_["ua_free"])
            sXL[0].inc(sync.dma_start(out=u2a_t[:, :, :], in_=u2a_d[:, :, base:base + T4B]))
            sb_["ua"][ti] = sXL[0].inc(sync.dma_start(out=x1bt[:, :, :],
                                                      in_=x1_d[:, :, base:base + T4B]))

        def emit_stats4(mc, ti=ti):
            @block.tensor
            def _(te, ti=ti, mc=mc):
                kq = mc % 4
                te.wait_ge(eV.h, sb_["sqb"][mc])
                te.wait_ge(eS.h, sb_[("u2c", mc)])
                if mc == 0 and sb_["sxrd"] is not None:
                    te.wait_ge(eV.h, sb_["sxrd"])
                te.matmul(psSx4[0:1, :], onesb[:, 0:1], u2bb[kq][:, :],
                          start=(mc == 0), stop=(mc == KC - 1), skip_group_check=True)
                m2 = te.matmul(psSxx4[0:1, :], onesb[:, 0:1], sq4[kq][:, :],
                               start=(mc == 0), stop=(mc == KC - 1), skip_group_check=True)
                mm = eT.inc(m2)
                sb_["u2b_rd"][kq] = mm
                if mc == KC - 1:
                    sb_["statmm"] = mm

        for mc in range(KC):
            slot = mc % 2
            @block.tensor
            def _(te, ti=ti, mc=mc, slot=slot, q=q):
                if mc == 0:
                    te.wait_ge(sG4[q].h, sb_["g"][ti])
                    if ti == 0:
                        te.wait_ge(eV.h, s4["tern"]["b"])
                prev = sb_["uslot"].get(slot)
                if prev is not None:
                    te.wait_ge(eV.h, prev)
                for idx, kc in enumerate(W2B_KCS):
                    kd = HW_LAST if kc == HKC - 1 else 128
                    ins = te.matmul(psU2[slot][:, :], t2B[0:kd, idx, mc * 128:(mc + 1) * 128],
                                    gtb[q][0:kd, idx, :], start=(idx == 0),
                                    stop=(idx == NB - 1))
                mm = eT.inc(ins)
                sb_["umm"][mc] = mm
                if mc == KC - 1:
                    sb_["g_free"][q] = mm

            @block.vector
            def _(vec, ti=ti, mc=mc, slot=slot):
                vec.wait_ge(eT.h, sb_["umm"][mc])
                if mc == 0:
                    vec.wait_ge(sXL[0].h, sb_["ua"][ti])
                    if sb_["u2rd"] is not None:
                        vec.wait_ge(eS.h, sb_["u2rd"])
                        vec.wait_ge(eV.h, sb_["u2rd_v"])
                vec.scalar_tensor_tensor(u2f[:, mc, :], psU2[slot][:, :], scal[:, 2:3],
                                         u2a_t[:, mc, :], op0=ALU.mult, op1=ALU.add)
                vec.drain()
                ins = vec.tensor_tensor(u2f[:, mc, :], u2f[:, mc, :], x1bt[:, mc, :],
                                        op=ALU.add)
                mk = eV.inc(ins)
                sb_["uslot"][slot] = mk
                sb_["u2"][mc] = mk
                if mc == KC - 1:
                    sb_["ua_free"] = mk

            @block.scalar
            def _(sc, ti=ti, mc=mc):
                kq = mc % 4
                sc.wait_ge(eV.h, sb_["u2"][mc])
                prev = sb_["u2b_rd"].get(kq)
                if prev is not None:
                    sc.wait_ge(eT.h, prev)
                sb_[("u2c", mc)] = eS.inc(sc.copy(u2bb[kq][:, :], u2f[:, mc, :]))

            @block.vector
            def _(vec, ti=ti, mc=mc):
                kq = mc % 4
                vec.wait_ge(eV.h, sb_["u2"][mc])
                prev = sb_["u2b_rd"].get(kq)
                if prev is not None:
                    vec.wait_ge(eT.h, prev)
                ins = vec.tensor_mul(sq4[kq][:, :], u2f[:, mc, :], u2f[:, mc, :])
                sb_["sqb"][mc] = eV.inc(ins)

            if mc >= 2:
                emit_stats4(mc - 2)
        emit_stats4(14)
        emit_stats4(15)

        @block.vector
        def _(vec, ti=ti):
            vec.wait_ge(eT.h, sb_["statmm"])
            vec.tensor_scalar(muinv[0:1, 0:T4B], psSx4[0:1, :], 1.0 / D, None, op0=ALU.mult)
            vec.tensor_scalar(n2s[0:1, 0:T4B], psSxx4[0:1, :], 1.0 / D, None, op0=ALU.mult)
            vec.drain()
            sb_["sxrd"] = eV.inc(vec.nop())
            vec.tensor_tensor(inv1[0:1, 0:T4B], muinv[0:1, 0:T4B], muinv[0:1, 0:T4B],
                              op=ALU.mult)
            vec.drain()
            vec.tensor_sub(n2s[0:1, 0:T4B], n2s[0:1, 0:T4B], inv1[0:1, 0:T4B])
            vec.drain()
            ins = vec.tensor_scalar(n2s[0:1, 0:T4B], n2s[0:1, 0:T4B], LN_EPS, None,
                                    op0=ALU.add)
            sb_["var"] = eV.inc(ins)

        @block.scalar
        def _(sc, ti=ti):
            sc.wait_ge(eV.h, sb_["var"])
            sb_["std"] = eS.inc(sc.activation(n2s[0:1, 0:T4B], n2s[0:1, 0:T4B], AF.Sqrt))

        @block.vector
        def _(vec, ti=ti):
            vec.wait_ge(eS.h, sb_["std"])
            vec.reciprocal(muinv[0:1, T4B:2 * T4B], n2s[0:1, 0:T4B])
            vec.drain()
            sb_["muinv"] = eV.inc(vec.nop())

        @block.tensor
        def _(te, ti=ti):
            te.wait_ge(eV.h, sb_["muinv"])
            if sb_["bcrd"] is not None:
                te.wait_ge(eV.h, sb_["bcrd"])
            sb_["bc"] = eT.inc(te.matmul(psb4[:, :], onesf[0:1, :], muinv[0:1, 0:2 * T4B],
                                         start=True, stop=True, skip_group_check=True))

        for mc in range(KC):
            @block.vector
            def _(vec, ti=ti, mc=mc):
                vec.wait_ge(eT.h, sb_["bc"])
                if mc == 0:
                    prev = sb_["outst"].get(ti - 1)
                    if prev is not None:
                        vec.wait_ge(sOUT.h, prev)
                vec.tensor_tensor(scr4[:, :], u2f[:, mc, :], psb4[:, 0:T4B],
                                  op=ALU.subtract)
                vec.drain()
                ins = vec.tensor_tensor(outt[:, mc, :], scr4[:, :], psb4[:, T4B:2 * T4B],
                                        op=ALU.mult)
                mk = eV.inc(ins)
                vec.drain()
                sb_["outv"][mc] = mk
                if mc == KC - 1:
                    sb_["bcrd"] = mk
                    sb_["u2rd_v"] = mk
                    sb_["u2rd"] = eS.n

        @block.scalar
        def _(sc, ti=ti, base=base):
            for mc in range(KC):
                sc.wait_ge(eV.h, sb_["outv"][mc])
            sb_["outst"][ti] = sOUT.inc(sc.dma_start(out=out_d[:, :, base:base + T4B],
                                                     in_=outt[:, :, :]))

    @block.sync
    def _(sync):
        for sem in ALL_SEMS:
            sync.wait_ge(sem.h, sem.n)

    p4b.close()
    ctx.close()
    return nc


PERM = np.concatenate([np.arange(0, 1024, 2), np.arange(1, 1024, 2),
                       np.arange(1024, 2048)])


def _featmajor(a):
    r, c = a.shape
    return np.ascontiguousarray(a.reshape(r // P, P, c).transpose(1, 0, 2))


def kernel(L0_attn_X, L0_attn_W_in, L0_attn_b_in, L0_attn_W_out, L0_attn_b_out,
           L0_ffn_w1, L0_ffn_w3, L0_ffn_w2):
    X = np.asarray(L0_attn_X)
    W_in = np.asarray(L0_attn_W_in); b_in = np.asarray(L0_attn_b_in)
    W_out = np.asarray(L0_attn_W_out); b_out = np.asarray(L0_attn_b_out)
    w1 = np.asarray(L0_ffn_w1); w3 = np.asarray(L0_ffn_w3); w2 = np.asarray(L0_ffn_w2)

    WQ = W_in[0:D][PERM]; WK = W_in[D:2 * D][PERM]; WV = W_in[2 * D:3 * D][PERM]
    wqh = _featmajor(np.ascontiguousarray(WQ.T))
    wkh = _featmajor(np.ascontiguousarray(WK.T))
    wvh = _featmajor(np.ascontiguousarray(WV.T))
    woh = _featmajor(np.ascontiguousarray(W_out[:, PERM].T))
    w1h = _featmajor(np.ascontiguousarray(w1.T))
    w3h = _featmajor(np.ascontiguousarray(w3.T))
    w2T = np.zeros((HKC * P, D), np.float32)
    w2T[0:H] = w2.T
    w2h = np.ascontiguousarray(w2T.reshape(HKC, P, D).transpose(1, 0, 2))

    bq = b_in[0:D][PERM].reshape(KC, P).T
    bk = b_in[D:2 * D][PERM].reshape(KC, P).T
    bv = b_in[2 * D:3 * D][PERM].reshape(KC, P).T
    bo = b_out.reshape(KC, P).T
    bqkv = np.ascontiguousarray(np.stack([bq, bk, bv, bo], axis=1)).astype(np.float32)

    xh_all = [np.ascontiguousarray(X[b].T.reshape(KC, P, S).transpose(1, 0, 2))
              for b in range(4)]

    nc = build_program()

    in_maps = []
    for c in range(8):
        b, h = c // 2, c % 2
        s0 = h * TOWN
        in_maps.append({
            "xh": xh_all[b],
            "xho": np.ascontiguousarray(xh_all[b][:, :, s0:s0 + TOWN]),
            "wqh": wqh, "wkh": wkh, "wvh": wvh, "woh": woh,
            "w1h": w1h, "w3h": w3h, "w2h": w2h,
            "bqkv": bqkv,
            "meta": np.full((P, 1), float(s0), np.float32),
        })
    res = run_bass_kernel_spmd(nc, in_maps, core_ids=list(range(8)))

    out = np.empty((4, S, D), np.float32)
    for c in range(8):
        b, h = c // 2, c % 2
        s0 = h * TOWN
        oc = np.asarray(res.results[c]["out"])
        out[b, s0:s0 + TOWN, :] = oc.transpose(1, 0, 2).reshape(D, TOWN).T
    return out


# version 08 (final)
# speedup vs baseline: 1.0762x; 1.0762x over previous
import contextlib
import numpy as np
import concourse.bass as bass
import concourse.mybir as mybir
from concourse.bass_utils import run_bass_kernel_spmd

F32 = mybir.dt.float32
BF16 = mybir.dt.bfloat16
I32 = mybir.dt.int32
AF = mybir.ActivationFunctionType
ALU = mybir.AluOpType
AX = mybir.AxisListType

P = 128
D = 2048
KC = 16
S = 8192
TOWN = 4096
H = 5461
HKC = 43
NW = H * D

T1 = 448
P1_TILES = [448] * 18 + [128]
T2 = 256
T3 = 512
T4A = 512
T4B = 256
SWEEPS = [(0, 8), (8, 8), (16, 8), (24, 8), (32, 8), (40, 3)]
HW_LAST = 85
W2A_KC = 22

TWO_PI = float(2 * np.pi)
PI_HI = float(np.float32(6.28125))
PI_LO = float(np.float32(TWO_PI - 6.28125))
LN_EPS = 1e-5
QLAD = [float(np.float32(10000.0 ** (-(2.0 ** b) / 512.0))) for b in range(9)]


def hw_of(hc):
    return HW_LAST if hc == HKC - 1 else 128


class Sem:
    def __init__(self, ctx, nc, name, step):
        self.h = ctx.enter_context(nc.semaphore(name))
        self.step = step
        self.n = 0

    def inc(self, instr):
        instr.then_inc(self.h, self.step)
        self.n += self.step
        return self.n


def build_program(phases=99, n1tiles=None):
    nc = bass.Bass("TRN2", target_bir_lowering=False)
    di = lambda name, shape, dt=F32: nc.dram_tensor(name, shape, dt, kind="ExternalInput").ap()
    xh = di("xh", [P, KC, S])
    xho = di("xho", [P, KC, TOWN])
    wqh = di("wqh", [P, KC, D]);  wkh = di("wkh", [P, KC, D])
    wvh = di("wvh", [P, KC, D]);  woh = di("woh", [P, KC, D])
    w1h = di("w1h", [P, KC, H]);  w3h = di("w3h", [P, KC, H])
    w2h = di("w2h", [P, HKC, D])
    bqkv = di("bqkv", [P, 4, KC])
    meta = di("meta", [P, 1])
    out_d = nc.dram_tensor("out", [P, KC, TOWN], F32, kind="ExternalOutput").ap()
    x1_d = nc.dram_tensor("x1_scr", [P, KC, TOWN], BF16, kind="Internal").ap()
    g_d = nc.dram_tensor("g_scr", [P, HKC, TOWN], BF16, kind="Internal").ap()
    u2a_d = nc.dram_tensor("u2a_scr", [P, KC, TOWN], F32, kind="Internal").ap()

    ctx = contextlib.ExitStack()
    sbg = lambda n, s, d=F32: ctx.enter_context(nc.sbuf_tensor(n, s, d))

    bias4 = sbg("bias4", [P, 4, KC])
    s0t = sbg("s0t", [P, 1])
    iota_t = sbg("iota_t", [P, 448])
    invf = sbg("invf", [P, 4])
    s0invf = sbg("s0invf", [P, 4])
    binv = sbg("binv", [P, 4])
    ji = sbg("ji", [P, 4], I32)
    jb = sbg("jb", [P, 4], I32)
    jf = sbg("jf", [P, 4])
    A = sbg("A", [P, KC])
    At = sbg("At", [P, KC])
    onesb = sbg("onesb", [P, 1], BF16)
    onesc = sbg("onesc", [P, 1])
    onesf = sbg("onesf", [1, P])
    halfpi = sbg("halfpi", [P, 1])
    scal = sbg("scal", [P, 12])
    sacc = sbg("sacc", [P, 3])
    acol = sbg("acol", [P, 1])
    wstr = [sbg("wstr0", [P, 512]), sbg("wstr1", [P, 512])]
    sA = sbg("sA", [1, 3])
    muinv = sbg("muinv", [1, 512])
    n2s = sbg("n2s", [1, 448])
    inv1 = sbg("inv1", [1, 448])

    sSET = Sem(ctx, nc, "sSET", 16)
    sAB = [Sem(ctx, nc, f"sAB{i}", 16) for i in range(2)]
    sWK = Sem(ctx, nc, "sWK", 16)
    sWV = Sem(ctx, nc, "sWV", 16)
    sWQ = Sem(ctx, nc, "sWQ", 16)
    sWO = Sem(ctx, nc, "sWO", 16)
    sXg = [Sem(ctx, nc, f"sXg{i}", 16) for i in range(2)]
    sXB = Sem(ctx, nc, "sXB", 16)
    sXR = Sem(ctx, nc, "sXR", 16)
    sTN = [Sem(ctx, nc, f"sTN{i}", 16) for i in range(2)]
    sG4 = [Sem(ctx, nc, f"sG4{i}", 16) for i in range(2)]
    sST = [Sem(ctx, nc, f"sST{i}", 16) for i in range(2)]
    sXL = [Sem(ctx, nc, f"sXL{i}", 16) for i in range(2)]
    sU = [Sem(ctx, nc, f"sU{i}", 16) for i in range(3)]
    sOUT = Sem(ctx, nc, "sOUT", 16)
    eT = Sem(ctx, nc, "eT", 1)
    eS = Sem(ctx, nc, "eS", 1)
    eV = Sem(ctx, nc, "eV", 1)
    eG = Sem(ctx, nc, "eG", 1)
    eP = Sem(ctx, nc, "eP", 1)

    block = ctx.enter_context(nc.Block())
    ALL_SEMS = [sSET, sWK, sWV, sWQ, sWO, sXB, sXR, sOUT, eT, eS, eV, eG] + \
        sAB + sXg + sTN + sG4 + sST + sXL + sU

    def barrier():
        @block.sync
        def _(sync):
            for sem in ALL_SEMS:
                sync.wait_ge(sem.h, sem.n)
            eP.inc(sync.nop())
        tgt = eP.n
        for name in ("tensor", "vector", "scalar", "gpsimd", "sync"):
            @getattr(block, name)
            def _(eng):
                eng.wait_ge(eP.h, tgt)

    @block.sync
    def _(sync):
        sSET.inc(sync.dma_start(out=bias4[:, :, :], in_=bqkv[:, :, :]))
        sSET.inc(sync.dma_start(out=s0t[:, :], in_=meta[:, :]))

    @block.gpsimd
    def _(g):
        g.iota(iota_t[:, :], [[1, 448]], channel_multiplier=0,
               allow_small_or_imprecise_dtypes=True)
        g.iota(ji[:, :], [[P, 4]], channel_multiplier=1,
               allow_small_or_imprecise_dtypes=True)
        g.memset(onesb[:, :], 1.0)
        g.memset(onesc[:, :], 1.0)
        g.memset(onesf[:, :], 1.0)
        g.memset(halfpi[:, :], float(np.pi / 2))
        g.memset(A[:, :], 0.0)
        g.memset(sacc[:, :], 0.0)
        g.memset(invf[:, :], 1.0)
        g.drain()
        eG.inc(g.nop())

    @block.vector
    def _(vec):
        vec.wait_ge(eG.h, eG.n)
        for b in range(9):
            vec.tensor_scalar(jb[:, :], ji[:, :], b, 1,
                              op0=ALU.logical_shift_right, op1=ALU.bitwise_and)
            vec.drain()
            vec.tensor_copy(jf[:, :], jb[:, :])
            vec.drain()
            vec.tensor_scalar(jf[:, :], jf[:, :], QLAD[b] - 1.0, 1.0,
                              op0=ALU.mult, op1=ALU.add)
            vec.drain()
            vec.tensor_mul(invf[:, :], invf[:, :], jf[:, :])
            vec.drain()
        vec.wait_ge(sSET.h, sSET.n)
        vec.tensor_scalar(s0invf[:, :], invf[:, :], s0t[:, 0:1], None, op0=ALU.mult)
        vec.drain()
        eV.inc(vec.nop())

    barrier()
    if phases < 1:
        @block.sync
        def _(sync):
            for sem in ALL_SEMS:
                sync.wait_ge(sem.h, sem.n)
        ctx.close()
        return nc

    absum_jobs = []
    for wsrc, col in ((w1h, 0), (w3h, 1), (w2h, 2)):
        for c in range(wsrc.shape[1]):
            for off in range(0, wsrc.shape[2], 512):
                absum_jobs.append((wsrc, c, off, min(512, wsrc.shape[2] - off), col))
    absM = {}
    absS = {"i": 0}

    def emit_absum(n):
        for _ in range(n):
            i = absS["i"]
            if i >= len(absum_jobs):
                return
            absS["i"] += 1
            wsrc, c, off, wdt, col = absum_jobs[i]
            q = i % 2

            @block.sync
            def _(sync, wsrc=wsrc, c=c, off=off, wdt=wdt, q=q, i=i):
                if i >= 2:
                    sync.wait_ge(eV.h, absM[(i - 2, "s")])
                absM[(i, "l")] = sAB[q].inc(sync.dma_start(out=wstr[q][:, 0:wdt],
                                                           in_=wsrc[:, c, off:off + wdt]))

            @block.vector
            def _(vec, wdt=wdt, q=q, col=col, i=i):
                vec.wait_ge(sAB[q].h, absM[(i, "l")])
                if i >= 1:
                    vec.wait_ge(eV.h, absM[(i - 1, "v")])
                ins = vec.reduce_sum(acol[:, 0:1], wstr[q][:, 0:wdt], axis=AX.X,
                                     apply_absolute_value=True)
                absM[(i, "s")] = eV.inc(ins)
                vec.wait_ge(eV.h, absM[(i, "s")])
                ins = vec.tensor_tensor(sacc[:, col:col + 1], sacc[:, col:col + 1],
                                        acol[:, 0:1], op=ALU.add)
                absM[(i, "v")] = eV.inc(ins)

    def emit_trig_base(base_imm, use_s0):
        @block.vector
        def _(vec):
            if use_s0:
                vec.scalar_tensor_tensor(binv[:, :], invf[:, :], float(base_imm),
                                         s0invf[:, :], op0=ALU.mult, op1=ALU.add)
            else:
                vec.tensor_scalar(binv[:, :], invf[:, :], float(base_imm), None,
                                  op0=ALU.mult)
            vec.drain()

    def emit_trig(tb, T, sinb, cosb, vmarks, smarks, chunks=(0, 1, 2, 3)):
        for c in chunks:
            @block.vector
            def _(vec, c=c):
                ang, kS, kC, rA, rB, ki = (tb["ang"], tb["kS"], tb["kC"],
                                           tb["rA"], tb["rB"], tb["ki"])
                if c > 0:
                    vec.wait_ge(eS.h, smarks[c - 1])
                vec.tensor_scalar(ang[:, 0:T], iota_t[:, 0:T], invf[:, c:c + 1],
                                  binv[:, c:c + 1], op0=ALU.mult, op1=ALU.add)
                vec.drain()
                vec.tensor_scalar(kS[:, 0:T], ang[:, 0:T], 1.0 / TWO_PI, None, op0=ALU.mult)
                vec.tensor_scalar(kC[:, 0:T], ang[:, 0:T], 1.0 / TWO_PI, 0.25,
                                  op0=ALU.mult, op1=ALU.add)
                vec.drain()
                vec.tensor_copy(ki[:, 0:T], kS[:, 0:T])
                vec.drain()
                vec.tensor_copy(kS[:, 0:T], ki[:, 0:T])
                vec.drain()
                vec.tensor_copy(ki[:, 0:T], kC[:, 0:T])
                vec.drain()
                vec.tensor_copy(kC[:, 0:T], ki[:, 0:T])
                vec.drain()
                vec.scalar_tensor_tensor(rA[:, 0:T], kS[:, 0:T], -PI_HI, ang[:, 0:T],
                                         op0=ALU.mult, op1=ALU.add)
                vec.scalar_tensor_tensor(rB[:, 0:T], kC[:, 0:T], -PI_HI, ang[:, 0:T],
                                         op0=ALU.mult, op1=ALU.add)
                vec.drain()
                vec.scalar_tensor_tensor(ang[:, 0:T], kC[:, 0:T], -PI_LO, rB[:, 0:T],
                                         op0=ALU.mult, op1=ALU.add)
                vec.drain()
                vec.scalar_tensor_tensor(rB[:, 0:T], kS[:, 0:T], -PI_LO, rA[:, 0:T],
                                         op0=ALU.mult, op1=ALU.add)
                vec.drain()
                vec.tensor_scalar(ang[:, 0:T], ang[:, 0:T], float(np.pi / 2), None,
                                  op0=ALU.add)
                vec.drain()
                vec.tensor_scalar(kC[:, 0:T], ang[:, 0:T], float(np.pi), None,
                                  op0=ALU.is_gt)
                vec.tensor_scalar(kS[:, 0:T], rB[:, 0:T], float(np.pi), None,
                                  op0=ALU.is_gt)
                vec.drain()
                vec.scalar_tensor_tensor(ang[:, 0:T], kC[:, 0:T], -TWO_PI, ang[:, 0:T],
                                         op0=ALU.mult, op1=ALU.add)
                vec.scalar_tensor_tensor(rB[:, 0:T], kS[:, 0:T], -TWO_PI, rB[:, 0:T],
                                         op0=ALU.mult, op1=ALU.add)
                vec.drain()
                vmarks[c] = eV.inc(vec.nop())

            @block.scalar
            def _(sc, c=c):
                sc.wait_ge(eV.h, vmarks[c])
                sc.activation(sinb[:, c, 0:T], tb["rB"][:, 0:T], AF.Sin)
                smarks[c] = eS.inc(sc.activation(cosb[:, c, 0:T], tb["ang"][:, 0:T],
                                                 AF.Sin))
        return eS.n

    p1 = contextlib.ExitStack()
    sb1 = lambda n, s, d=F32: p1.enter_context(nc.sbuf_tensor(n, s, d))
    wkA = sb1("wkA", [P, KC, D], BF16)
    wvA = sb1("wvA", [P, KC, D], BF16)
    xb = [sb1("xb0", [P, KC, T1], BF16), sb1("xb1", [P, KC, T1], BF16)]
    Kt = sb1("Kt", [P, KC, T1], BF16)
    k2b = [sb1(f"k2b{i}", [P, T1], BF16) for i in range(4)]
    sinb1 = sb1("sinb1", [P, 4, T1], BF16)
    cosb1 = sb1("cosb1", [P, 4, T1], BF16)
    tb1 = {k: sb1("t1" + k, [P, T1], I32 if k == "ki" else F32)
           for k in ("ang", "kS", "kC", "rA", "rB", "ki")}
    psK = [p1.enter_context(nc.psum_tensor(f"psK{i}", [P, T1], F32)) for i in range(2)]
    psV = [p1.enter_context(nc.psum_tensor(f"psV{i}", [P, T1], F32)) for i in range(4)]
    psn = p1.enter_context(nc.psum_tensor("psn", [1, T1], F32))
    psb = p1.enter_context(nc.psum_tensor("psb", [P, T1], F32))

    @block.gpsimd
    def _(g):
        for kc in range(KC):
            sWK.inc(g.dma_start(out=wkA[:, kc, :], in_=wkh[:, kc, :]))
        for kc in range(KC):
            sWV.inc(g.dma_start(out=wvA[:, kc, :], in_=wvh[:, kc, :]))

    s1 = {"xb": {}, "xb_free": {}, "kslot": {}, "vslot": {}, "kt_rd": {},
          "k2_rd": {}, "psn_rd": None, "psb_rd": None, "psb_mm": None,
          "evac": {}, "sq": {}, "vmm": {}, "n2mm": None, "n2sqrt": None,
          "n2inv": None, "trig": None}

    pos = 0
    p1_tiles = P1_TILES if n1tiles is None else P1_TILES[:n1tiles]
    for ti, T in enumerate(p1_tiles):
        base = pos
        pos += T
        q = ti % 2

        @block.gpsimd
        def _(g, ti=ti, q=q, base=base, T=T):
            if ti >= 2:
                g.wait_ge(eT.h, s1["xb_free"][q])
            sXg[q].inc(g.dma_start(out=xb[q][:, 0:8, 0:T], in_=xh[:, 0:8, base:base + T]))
            s1["xb"][ti] = sXg[q].inc(g.dma_start(out=xb[q][:, 8:16, 0:T],
                                                  in_=xh[:, 8:16, base:base + T]))

        emit_trig_base(base, False)
        tg1 = {"v": {}, "s": {}}

        def emit_n2(mc, ti=ti, T=T):
            @block.tensor
            def _(te, mc=mc, ti=ti, T=T):
                te.wait_ge(eS.h, s1["sq"][mc])
                if mc == 0 and s1["psn_rd"] is not None:
                    te.wait_ge(eS.h, s1["psn_rd"])
                mm = eT.inc(te.matmul(psn[0:1, 0:T], onesb[:, 0:1], k2b[mc % 4][:, 0:T],
                                      start=(mc == 0), stop=(mc == KC - 1),
                                      skip_group_check=True))
                s1["k2_rd"][mc % 4] = mm
                if mc == KC - 1:
                    s1["n2mm"] = mm

        for mc in range(KC):
            slot = mc % 2
            @block.tensor
            def _(te, ti=ti, mc=mc, slot=slot, q=q, T=T):
                if mc == 0:
                    te.wait_ge(sXg[q].h, s1["xb"][ti])
                    if ti == 0:
                        te.wait_ge(sWK.h, sWK.n)
                prev = s1["kslot"].get(slot)
                if prev is not None:
                    te.wait_ge(eV.h, prev)
                for kc in range(KC):
                    ins = te.matmul(psK[slot][:, 0:T], wkA[:, kc, mc * 128:(mc + 1) * 128],
                                    xb[q][:, kc, 0:T], start=(kc == 0), stop=(kc == KC - 1))
                s1["evac"][("mm", mc)] = eT.inc(ins)

            @block.vector
            def _(vec, ti=ti, mc=mc, slot=slot, T=T):
                vec.wait_ge(eT.h, s1["evac"][("mm", mc)])
                ins = vec.tensor_scalar(Kt[:, mc, 0:T], psK[slot][:, 0:T],
                                        bias4[:, 1, mc:mc + 1], None, op0=ALU.add)
                ev1 = eV.inc(ins)
                s1["kslot"][slot] = ev1
                s1["evac"][mc] = ev1

            @block.scalar
            def _(sc, ti=ti, mc=mc, T=T):
                sc.wait_ge(eV.h, s1["evac"][mc])
                prevn2 = s1["k2_rd"].get(mc % 4)
                if prevn2 is not None:
                    sc.wait_ge(eT.h, prevn2)
                s1["sq"][mc] = eS.inc(sc.activation(k2b[mc % 4][:, 0:T], Kt[:, mc, 0:T],
                                                    AF.Square))
            if mc >= 2:
                emit_n2(mc - 2)
            if mc in (2, 5, 8, 11):
                s1["trig"] = emit_trig(tb1, T, sinb1, cosb1, tg1["v"], tg1["s"],
                                       chunks=((mc - 2) // 3,))
        emit_n2(14)
        emit_n2(15)

        @block.scalar
        def _(sc, ti=ti, T=T):
            sc.wait_ge(eT.h, s1["n2mm"])
            mk = eS.inc(sc.activation(n2s[0:1, 0:T], psn[0:1, 0:T], AF.Sqrt))
            s1["psn_rd"] = mk
            s1["n2sqrt"] = mk

        @block.vector
        def _(vec, ti=ti, T=T):
            vec.wait_ge(eS.h, s1["n2sqrt"])
            vec.tensor_scalar(n2s[0:1, 0:T], n2s[0:1, 0:T], 1e-12, None, op0=ALU.max)
            vec.drain()
            vec.reciprocal(inv1[0:1, 0:T], n2s[0:1, 0:T])
            vec.drain()
            s1["n2inv"] = eV.inc(vec.nop())

        def emit_sttc(c, ti=ti, T=T):
            @block.vector
            def _(vec, c=c, ti=ti, T=T):
                if c == 0:
                    vec.wait_ge(eT.h, s1["psb_mm"])
                for par in (0, 1):
                    prevd = s1.get(("sttdump", par))
                    if prevd is not None:
                        vec.wait_ge(eV.h, prevd)
                if c >= 8:
                    vec.wait_ge(eS.h, s1["sq"][c])
                vec.tensor_tensor(Kt[:, c, 0:T], Kt[:, c, 0:T], psb[:, 0:T],
                                  op=ALU.mult)
                vec.drain()
                vec.wait_ge(eT.h, s1["vmm"][c])
                dump = tb1["rA"] if c % 2 == 0 else tb1["rB"]
                ins = vec.scalar_tensor_tensor(dump[:, 0:T], psV[c % 4][:, 0:T],
                                               bias4[:, 2, c:c + 1], Kt[:, c, 0:T],
                                               op0=ALU.add, op1=ALU.mult,
                                               accum_out=At[:, c:c + 1])
                mk = eV.inc(ins)
                s1[("sttdump", c % 2)] = mk
                s1["vslot"][c % 4] = mk
                s1["kt_rd"][c] = mk
                if c == KC - 1:
                    s1["psb_rd"] = mk

        for mc in range(KC):
            slot = mc % 4
            if mc == 4:
                @block.tensor
                def _(te, ti=ti, T=T):
                    te.wait_ge(eV.h, s1["n2inv"])
                    if s1["psb_rd"] is not None:
                        te.wait_ge(eV.h, s1["psb_rd"])
                    s1["psb_mm"] = eT.inc(te.matmul(psb[:, 0:T], onesf[0:1, :],
                                                    inv1[0:1, 0:T], start=True, stop=True,
                                                    skip_group_check=True))
            if mc >= 4:
                emit_sttc(mc - 4)
            @block.tensor
            def _(te, ti=ti, mc=mc, slot=slot, q=q, T=T):
                prev = s1["vslot"].get(slot)
                if prev is not None:
                    te.wait_ge(eV.h, prev)
                if ti == 0 and mc == 0:
                    te.wait_ge(sWV.h, sWV.n)
                for kc in range(KC):
                    ins = te.matmul(psV[slot][:, 0:T], wvA[:, kc, mc * 128:(mc + 1) * 128],
                                    xb[q][:, kc, 0:T], start=(kc == 0), stop=(kc == KC - 1))
                mm = eT.inc(ins)
                s1["vmm"][mc] = mm
                if mc == KC - 1:
                    s1["xb_free"][q] = mm

            if mc < 4:
                @block.vector
                def _(vec, ti=ti, mc=mc, T=T):
                    ang, kS, kC, rA = tb1["ang"], tb1["kS"], tb1["kC"], tb1["rA"]
                    c = mc
                    vec.wait_ge(eS.h, tg1["s"][c])
                    vec.wait_ge(eS.h, s1["sq"][c])
                    vec.wait_ge(eS.h, s1["sq"][c + 4])
                    for par in (0, 1):
                        prevd = s1.get(("sttdump", par))
                        if prevd is not None:
                            vec.wait_ge(eV.h, prevd)
                    vec.tensor_tensor(ang[:, 0:T], Kt[:, c, 0:T], cosb1[:, c, 0:T],
                                      op=ALU.mult)
                    vec.tensor_tensor(kS[:, 0:T], Kt[:, c + 4, 0:T], sinb1[:, c, 0:T],
                                      op=ALU.mult)
                    vec.tensor_tensor(kC[:, 0:T], Kt[:, c, 0:T], sinb1[:, c, 0:T],
                                      op=ALU.mult)
                    vec.tensor_tensor(rA[:, 0:T], Kt[:, c + 4, 0:T], cosb1[:, c, 0:T],
                                      op=ALU.mult)
                    vec.drain()
                    vec.tensor_sub(Kt[:, c, 0:T], ang[:, 0:T], kS[:, 0:T])
                    vec.tensor_add(Kt[:, c + 4, 0:T], rA[:, 0:T], kC[:, 0:T])
                    vec.drain()
                    eV.inc(vec.nop())

        for c in (12, 13, 14, 15):
            emit_sttc(c)

        @block.vector
        def _(vec, ti=ti, T=T):
            vec.drain()
            vec.tensor_add(A[:, :], A[:, :], At[:, :])
            vec.drain()
            eV.inc(vec.nop())

        emit_absum(13)

    barrier()
    p1.close()
    if phases < 2:
        @block.sync
        def _(sync):
            for sem in ALL_SEMS:
                sync.wait_ge(sem.h, sem.n)
        ctx.close()
        return nc

    p2 = contextlib.ExitStack()
    sb2 = lambda n, s, d=F32: p2.enter_context(nc.sbuf_tensor(n, s, d))
    wqA = sb2("wqA", [P, KC, D], BF16)
    woA = sb2("woA", [P, KC, D], BF16)
    xb2 = sb2("xb2", [P, KC, T2], BF16)
    xr2 = sb2("xr2", [P, KC, T2], BF16)
    Qt = sb2("Qt", [P, 8, T2], BF16)
    Ybf = sb2("Ybf", [P, KC, T2], BF16)
    u1 = sb2("u1", [P, KC, T2])
    u1b = [sb2(f"u1b{i}", [P, T2], BF16) for i in range(4)]
    sqb = [sb2(f"sqb{i}", [P, T2], BF16) for i in range(4)]
    x1full = sb2("x1full", [P, KC, T2], BF16)
    sinb2 = sb2("sinb2", [P, 4, T2], BF16)
    cosb2 = sb2("cosb2", [P, 4, T2], BF16)
    tb2 = {k: sb2("t2" + k, [P, T2], I32 if k == "ki" else F32)
           for k in ("ang", "kS", "kC", "rA", "rB", "ki")}
    psQ = [p2.enter_context(nc.psum_tensor(f"psQ{i}", [P, T2], F32)) for i in range(2)]
    psO = [p2.enter_context(nc.psum_tensor(f"psO{i}", [P, T2], F32)) for i in range(2)]
    psSx = p2.enter_context(nc.psum_tensor("psSx", [1, T2], F32))
    psSxx = p2.enter_context(nc.psum_tensor("psSxx", [1, T2], F32))
    psb2 = p2.enter_context(nc.psum_tensor("psb2", [P, 2 * T2], F32))

    s2 = {"xb": {}, "xr": {}, "qslot": {}, "oslot": {}, "qe": {}, "omm": {},
          "u1": {}, "u1rd": None, "u1rd_v": None, "statmm": None, "bc": None,
          "bcrd": None, "u1b_rd": {}, "x1st": {}, "x1v": {}, "sxrd": None,
          "xb_free": None, "xb_free2": {}, "xr_free": None, "ybf_free": None, "wo_last": None,
          "qt_rd": {}, "qt": {}, "ybf": {}, "sqb": {}, "trig": None,
          "var": None, "std": None, "muinv": None}

    @block.gpsimd
    def _(g):
        for kc in range(KC):
            sWQ.inc(g.dma_start(out=wqA[:, kc, :], in_=wqh[:, kc, :]))

    N2TILES = TOWN // T2

    def emit_x1(ti):
        base = ti * T2
        bc = s2[("bc", ti)]
        for mc in range(KC):
            @block.vector
            def _(vec, ti=ti, mc=mc, bc=bc):
                vec.wait_ge(eT.h, bc)
                if mc == 0:
                    vec.wait_ge(eS.h, s2["trig"])
                    prev = s2["x1st"].get(0)
                    if prev is not None:
                        vec.wait_ge(sST[0].h, prev)
                vec.tensor_tensor(tb2["ang"][:, :], u1[:, mc, :], psb2[:, 0:T2],
                                  op=ALU.subtract)
                vec.drain()
                ins = vec.tensor_tensor(x1full[:, mc, :], tb2["ang"][:, :],
                                        psb2[:, T2:2 * T2], op=ALU.mult)
                mk = eV.inc(ins)
                vec.drain()
                s2["x1v"][mc] = mk
                if mc == KC - 1:
                    s2["bcrd"] = mk
                    s2["u1rd_v"] = mk
                    s2["u1rd"] = eS.n

        @block.sync
        def _(sync, ti=ti, base=base):
            for mc in range(KC):
                sync.wait_ge(eV.h, s2["x1v"][mc])
            s2["x1st"][0] = sST[0].inc(
                sync.dma_start(out=x1_d[:, :, base:base + T2], in_=x1full[:, :, :]))

    for ti in range(N2TILES):
        base = ti * T2

        @block.gpsimd
        def _(g, ti=ti, base=base):
            if ti > 0:
                g.wait_ge(eT.h, s2["xb_free"])
            sXB.inc(g.dma_start(out=xb2[:, 0:8, :], in_=xho[:, 0:8, base:base + T2]))
            s2["xb"][ti] = sXB.inc(g.dma_start(out=xb2[:, 8:16, :],
                                               in_=xho[:, 8:16, base:base + T2]))
            if ti > 0:
                g.wait_ge(eV.h, s2["xr_free"])
            sXR.inc(g.dma_start(out=xr2[:, 0:8, :], in_=xho[:, 0:8, base:base + T2]))
            s2["xr"][ti] = sXR.inc(g.dma_start(out=xr2[:, 8:16, :],
                                               in_=xho[:, 8:16, base:base + T2]))

        if ti == 0:
            @block.gpsimd
            def _(g):
                for kc in range(KC):
                    sWO.inc(g.dma_start(out=woA[:, kc, :], in_=woh[:, kc, :]))

        emit_trig_base(base, True)
        tg2 = {"v": {}, "s": {}}

        for mc in range(KC):
            slot = mc % 2
            @block.tensor
            def _(te, ti=ti, mc=mc, slot=slot):
                if mc == 0:
                    te.wait_ge(sXB.h, s2["xb"][ti])
                    if ti == 0:
                        te.wait_ge(sWQ.h, sWQ.n)
                prev = s2["qslot"].get(slot)
                if prev is not None:
                    eng, mk = prev
                    te.wait_ge(eng.h, mk)
                for kc in range(KC):
                    ins = te.matmul(psQ[slot][:, :], wqA[:, kc, mc * 128:(mc + 1) * 128],
                                    xb2[:, kc, :], start=(kc == 0), stop=(kc == KC - 1))
                mm = eT.inc(ins)
                s2["qe"][mc] = mm
                if mc == KC - 1:
                    s2["xb_free"] = mm

            if mc < 8:
                @block.vector
                def _(vec, ti=ti, mc=mc, slot=slot):
                    vec.wait_ge(eT.h, s2["qe"][mc])
                    ins = vec.tensor_scalar(Qt[:, mc, :], psQ[slot][:, :],
                                            bias4[:, 0, mc:mc + 1], None, op0=ALU.add)
                    mk = eV.inc(ins)
                    vec.drain()
                    s2["qslot"][slot] = (eV, mk)
                    s2["qt"][mc] = mk
            else:
                @block.vector
                def _(vec, ti=ti, mc=mc, slot=slot):
                    vec.wait_ge(eT.h, s2["qe"][mc])
                    if ti > 0 and mc == 8:
                        vec.wait_ge(eT.h, s2["ybf_free"])
                    ins = vec.tensor_scalar(Ybf[:, mc, :], psQ[slot][:, :],
                                            bias4[:, 0, mc:mc + 1], A[:, mc:mc + 1],
                                            op0=ALU.add, op1=ALU.mult)
                    mk = eV.inc(ins)
                    s2["qslot"][slot] = (eV, mk)
                    s2["ybf"][mc] = (eV, mk)
            if mc % 4 == 3:
                s2["trig"] = emit_trig(tb2, T2, sinb2, cosb2, tg2["v"], tg2["s"],
                                       chunks=(mc // 4,))

        @block.vector
        def _(vec, ti=ti):
            ang, kS, kC, rA = tb2["ang"], tb2["kS"], tb2["kC"], tb2["rA"]
            vec.wait_ge(eS.h, s2["trig"])
            if ti > 0:
                vec.wait_ge(eT.h, s2["ybf_free"])
            for c in range(4):
                vec.wait_ge(eV.h, s2["qt"][c])
                vec.wait_ge(eV.h, s2["qt"][c + 4])
                vec.tensor_tensor(ang[:, :], Qt[:, c, :], cosb2[:, c, :], op=ALU.mult)
                vec.tensor_tensor(kS[:, :], Qt[:, c + 4, :], sinb2[:, c, :], op=ALU.mult)
                vec.tensor_tensor(kC[:, :], Qt[:, c, :], sinb2[:, c, :], op=ALU.mult)
                vec.tensor_tensor(rA[:, :], Qt[:, c + 4, :], cosb2[:, c, :], op=ALU.mult)
                vec.drain()
                vec.tensor_sub(ang[:, :], ang[:, :], kS[:, :])
                vec.tensor_add(rA[:, :], rA[:, :], kC[:, :])
                vec.drain()
                vec.tensor_scalar(Ybf[:, c, :], ang[:, :], A[:, c:c + 1], None, op0=ALU.mult)
                vec.tensor_scalar(Ybf[:, c + 4, :], rA[:, :], A[:, c + 4:c + 5], None,
                                  op0=ALU.mult)
                vec.drain()
                mk = eV.inc(vec.nop())
                s2["qt_rd"][c] = mk
                s2["qt_rd"][c + 4] = mk
                s2["ybf"][c] = (eV, mk)
                s2["ybf"][c + 4] = (eV, mk)

        if ti > 0:
            emit_x1(ti - 1)

        def emit_stats2(mc, ti=ti):
            @block.tensor
            def _(te, ti=ti, mc=mc):
                kq = mc % 4
                te.wait_ge(eV.h, s2["sqb"][mc])
                te.wait_ge(eS.h, s2[("u1c", mc)])
                if mc == 0 and s2["sxrd"] is not None:
                    te.wait_ge(eV.h, s2["sxrd"])
                te.matmul(psSx[0:1, :], onesb[:, 0:1], u1b[kq][:, :],
                          start=(mc == 0), stop=(mc == KC - 1), skip_group_check=True)
                m2 = te.matmul(psSxx[0:1, :], onesb[:, 0:1], sqb[kq][:, :],
                               start=(mc == 0), stop=(mc == KC - 1), skip_group_check=True)
                mm = eT.inc(m2)
                s2["u1b_rd"][kq] = mm
                if mc == KC - 1:
                    s2["statmm"] = mm

        for mc in range(KC):
            slot = mc % 2
            @block.tensor
            def _(te, ti=ti, mc=mc, slot=slot):
                if mc == 0:
                    for c in range(KC):
                        eng, mk = s2["ybf"][c]
                        te.wait_ge(eng.h, mk)
                    if ti == 0:
                        te.wait_ge(sWO.h, sWO.n)
                prev = s2["oslot"].get(slot)
                if prev is not None:
                    te.wait_ge(eV.h, prev)
                for kc in range(KC):
                    ins = te.matmul(psO[slot][:, :], woA[:, kc, mc * 128:(mc + 1) * 128],
                                    Ybf[:, kc, :], start=(kc == 0), stop=(kc == KC - 1))
                mm = eT.inc(ins)
                s2["omm"][mc] = mm
                if mc == KC - 1:
                    s2["ybf_free"] = mm

            @block.vector
            def _(vec, ti=ti, mc=mc, slot=slot):
                vec.wait_ge(eT.h, s2["omm"][mc])
                if mc == 0:
                    vec.wait_ge(sXR.h, s2["xr"][ti])
                    if s2["u1rd"] is not None:
                        vec.wait_ge(eS.h, s2["u1rd"])
                        vec.wait_ge(eV.h, s2["u1rd_v"])
                vec.scalar_tensor_tensor(u1[:, mc, :], xr2[:, mc, :], 2.0, psO[slot][:, :],
                                         op0=ALU.mult, op1=ALU.add)
                vec.drain()
                ins = vec.tensor_scalar(u1[:, mc, :], u1[:, mc, :], bias4[:, 3, mc:mc + 1],
                                        None, op0=ALU.add)
                mk = eV.inc(ins)
                s2["oslot"][slot] = mk
                s2["u1"][mc] = mk
                if mc == KC - 1:
                    s2["xr_free"] = mk

            @block.scalar
            def _(sc, ti=ti, mc=mc):
                kq = mc % 4
                sc.wait_ge(eV.h, s2["u1"][mc])
                prev = s2["u1b_rd"].get(kq)
                if prev is not None:
                    sc.wait_ge(eT.h, prev)
                s2[("u1c", mc)] = eS.inc(sc.copy(u1b[kq][:, :], u1[:, mc, :]))

            @block.vector
            def _(vec, ti=ti, mc=mc):
                kq = mc % 4
                vec.wait_ge(eV.h, s2["u1"][mc])
                prev = s2["u1b_rd"].get(kq)
                if prev is not None:
                    vec.wait_ge(eT.h, prev)
                ins = vec.tensor_mul(sqb[kq][:, :], u1[:, mc, :], u1[:, mc, :])
                s2["sqb"][mc] = eV.inc(ins)

            if mc >= 2:
                emit_stats2(mc - 2)
        emit_stats2(14)
        emit_stats2(15)

        @block.vector
        def _(vec, ti=ti):
            vec.wait_ge(eT.h, s2["statmm"])
            vec.tensor_scalar(muinv[0:1, 0:T2], psSx[0:1, :], 1.0 / D, None, op0=ALU.mult)
            vec.tensor_scalar(n2s[0:1, 0:T2], psSxx[0:1, :], 1.0 / D, None, op0=ALU.mult)
            vec.drain()
            s2["sxrd"] = eV.inc(vec.nop())
            vec.tensor_tensor(inv1[0:1, 0:T2], muinv[0:1, 0:T2], muinv[0:1, 0:T2],
                              op=ALU.mult)
            vec.drain()
            vec.tensor_sub(n2s[0:1, 0:T2], n2s[0:1, 0:T2], inv1[0:1, 0:T2])
            vec.drain()
            ins = vec.tensor_scalar(n2s[0:1, 0:T2], n2s[0:1, 0:T2], LN_EPS, None, op0=ALU.add)
            s2["var"] = eV.inc(ins)

        @block.scalar
        def _(sc, ti=ti):
            sc.wait_ge(eV.h, s2["var"])
            s2["std"] = eS.inc(sc.activation(n2s[0:1, 0:T2], n2s[0:1, 0:T2], AF.Sqrt))

        @block.vector
        def _(vec, ti=ti):
            vec.wait_ge(eS.h, s2["std"])
            vec.reciprocal(muinv[0:1, T2:2 * T2], n2s[0:1, 0:T2])
            vec.drain()
            s2["muinv"] = eV.inc(vec.nop())

        @block.tensor
        def _(te, ti=ti):
            te.wait_ge(eV.h, s2["muinv"])
            if s2["bcrd"] is not None:
                te.wait_ge(eV.h, s2["bcrd"])
            s2[("bc", ti)] = eT.inc(te.matmul(psb2[:, :], onesf[0:1, :], muinv[0:1, :],
                                              start=True, stop=True,
                                              skip_group_check=True))

        emit_absum(18)

    emit_x1(N2TILES - 1)

    emit_absum(len(absum_jobs))

    @block.tensor
    def _(te):
        te.wait_ge(eV.h, eV.n)
        te.wait_ge(eS.h, eS.n)
        eT.inc(te.matmul(psSx[0:1, 0:3], onesc[:, 0:1], sacc[:, :], start=True, stop=True,
                         skip_group_check=True))

    @block.vector
    def _(vec):
        vec.wait_ge(eT.h, eT.n)
        vec.tensor_scalar(sA[0:1, :], psSx[0:1, 0:3], 1.0 / float(NW), None, op0=ALU.mult)
        vec.drain()
        eV.inc(vec.nop())

    @block.tensor
    def _(te):
        te.wait_ge(eV.h, eV.n)
        eT.inc(te.matmul(psb2[:, 0:3], onesf[0:1, :], sA[0:1, :], start=True, stop=True,
                         skip_group_check=True))

    @block.vector
    def _(vec):
        vec.wait_ge(eT.h, eT.n)
        vec.tensor_copy(scal[:, 0:3], psb2[:, 0:3])
        vec.drain()
        vec.tensor_scalar(scal[:, 3:6], scal[:, 0:3], 0.5, None, op0=ALU.mult)
        vec.tensor_scalar(scal[:, 6:9], scal[:, 0:3], -0.5, None, op0=ALU.mult)
        vec.drain()
        eV.inc(vec.nop())

    barrier()
    p2.close()
    if phases < 3:
        @block.sync
        def _(sync):
            for sem in ALL_SEMS:
                sync.wait_ge(sem.h, sem.n)
        ctx.close()
        return nc

    p3 = contextlib.ExitStack()
    sb3 = lambda n, s, d=F32: p3.enter_context(nc.sbuf_tensor(n, s, d))
    t1A = [sb3("t1A0", [P, KC, 1024], BF16), sb3("t1A1", [P, KC, 1024], BF16)]
    t3A = [sb3("t3A0", [P, KC, 1024], BF16), sb3("t3A1", [P, KC, 1024], BF16)]
    x1t = [sb3("x1t0", [P, KC, T3], BF16), sb3("x1t1", [P, KC, T3], BF16)]
    gt = [sb3("gt0", [P, 8, T3], BF16), sb3("gt1", [P, 8, T3], BF16)]
    silb = sb3("silb", [P, T3])
    wst3 = [sb3("wst30", [P, 1024]), sb3("wst31", [P, 1024])]
    psH1 = [p3.enter_context(nc.psum_tensor(f"psH1{i}", [P, T3], F32)) for i in range(2)]
    psH3 = [p3.enter_context(nc.psum_tensor(f"psH3{i}", [P, T3], F32)) for i in range(2)]

    s3 = {"x1": {}, "x1_free": {}, "h1slot": {}, "h3slot": {}, "g_rd": {},
          "arena_tern": {}, "arena_free": {}, "strip": {}, "strip_g": {},
          "sil_rd": None, "h1": {}, "h3": {}, "gv": {}}

    def emit_tern_arena(j, lo=0, hi=32):
        hc0, nhc = SWEEPS[j]
        hoff = hc0 * 128
        hlen = (H - hoff) if j == len(SWEEPS) - 1 else nhc * 128
        par = j % 2
        jobs = [(w1h, t1A[par], 3, 6, kc) for kc in range(KC)] + \
               [(w3h, t3A[par], 4, 7, kc) for kc in range(KC)]
        for idx, (wsrc, dstA, colp, coln, kc) in enumerate(jobs):
            if not (lo <= idx < hi):
                continue
            q = idx % 2
            @block.gpsimd
            def _(g, wsrc=wsrc, kc=kc, q=q, idx=idx, j=j, par=par,
                  hoff=hoff, hlen=hlen):
                if idx == 0 and j >= 2:
                    g.wait_ge(eT.h, s3["arena_free"][par])
                prevs = s3["strip_g"].get(q)
                if prevs is not None:
                    g.wait_ge(eV.h, prevs)
                sTN[q].inc(g.dma_start(out=wst3[q][:, 0:hlen],
                                       in_=wsrc[:, kc, hoff:hoff + hlen]))

            @block.vector
            def _(vec, dstA=dstA, colp=colp, coln=coln, kc=kc, q=q,
                  idx=idx, j=j, hlen=hlen, njobs=len(jobs)):
                vec.wait_ge(sTN[q].h, sTN[q].n)
                vec.tensor_scalar(dstA[:, kc, 0:hlen], wst3[q][:, 0:hlen],
                                  scal[:, colp:colp + 1], None, op0=ALU.is_gt)
                vec.drain()
                vec.tensor_scalar(wst3[q][:, 0:hlen], wst3[q][:, 0:hlen],
                                  scal[:, coln:coln + 1], None, op0=ALU.is_lt)
                vec.drain()
                ins = vec.tensor_tensor(dstA[:, kc, 0:hlen], dstA[:, kc, 0:hlen],
                                        wst3[q][:, 0:hlen], op=ALU.subtract)
                mk = eV.inc(ins)
                vec.drain()
                s3["strip"][(j, idx)] = mk
                s3["strip_g"][q] = mk
                if idx == njobs - 1:
                    s3["arena_tern"][j] = mk

    emit_tern_arena(0)

    N3TILES = TOWN // T3
    gtile = 0
    for j, (hc0, nhc) in enumerate(SWEEPS):
        par = j % 2
        for ti in range(N3TILES):
            base = ti * T3
            q = gtile % 2
            if j + 1 < len(SWEEPS):
                emit_tern_arena(j + 1, lo=4 * ti, hi=4 * ti + 4)

            @block.sync
            def _(sync, base=base, q=q, gtile=gtile):
                prev = s3["x1_free"].get(q)
                if prev is not None:
                    sync.wait_ge(eT.h, prev)
                s3["x1"][gtile] = sXL[q].inc(sync.dma_start(out=x1t[q][:, :, :],
                                                            in_=x1_d[:, :, base:base + T3]))

            for i in range(nhc):
                hw = hw_of(hc0 + i)
                hq = i % 2
                @block.tensor
                def _(te, j=j, ti=ti, i=i, hw=hw, hq=hq, q=q, par=par, gtile=gtile):
                    if i == 0:
                        te.wait_ge(sXL[q].h, s3["x1"][gtile])
                        if ti == 0:
                            te.wait_ge(eV.h, s3["arena_tern"][j])
                    prev = s3["h1slot"].get(hq)
                    if prev is not None:
                        te.wait_ge(eS.h, prev)
                    for kc in range(KC):
                        ins = te.matmul(psH1[hq][0:hw, :],
                                        t1A[par][:, kc, i * 128:i * 128 + hw],
                                        x1t[q][:, kc, :], start=(kc == 0), stop=(kc == KC - 1))
                    s3["h1"][i] = eT.inc(ins)
                    prev = s3["h3slot"].get(hq)
                    if prev is not None:
                        te.wait_ge(eV.h, prev)
                    for kc in range(KC):
                        ins = te.matmul(psH3[hq][0:hw, :],
                                        t3A[par][:, kc, i * 128:i * 128 + hw],
                                        x1t[q][:, kc, :], start=(kc == 0), stop=(kc == KC - 1))
                    mm = eT.inc(ins)
                    s3["h3"][i] = mm
                    if i == nhc - 1:
                        s3["x1_free"][q] = mm
                        if ti == N3TILES - 1:
                            s3["arena_free"][par] = mm

                @block.scalar
                def _(sc, i=i, hw=hw, hq=hq):
                    sc.wait_ge(eT.h, s3["h1"][i])
                    if s3["sil_rd"] is not None:
                        sc.wait_ge(eV.h, s3["sil_rd"])
                    mk = eS.inc(sc.activation(silb[0:hw, :], psH1[hq][0:hw, :], AF.Silu,
                                              scale=scal[0:hw, 0:1]))
                    s3["h1slot"][hq] = mk
                    s3["sil"] = mk

                @block.vector
                def _(vec, i=i, hw=hw, hq=hq, q=q):
                    vec.wait_ge(eS.h, s3["sil"])
                    vec.wait_ge(eT.h, s3["h3"][i])
                    prev = s3["g_rd"].get((q, i))
                    if prev is not None:
                        vec.wait_ge(sST[q].h, prev)
                    ins = vec.scalar_tensor_tensor(gt[q][0:hw, i, :], psH3[hq][0:hw, :],
                                                   scal[0:hw, 1:2], silb[0:hw, :],
                                                   op0=ALU.mult, op1=ALU.mult)
                    mk = eV.inc(ins)
                    s3["h3slot"][hq] = mk
                    s3["gv"][i] = mk
                    s3["sil_rd"] = mk

            @block.sync
            def _(sync, base=base, q=q, nhc=nhc, hc0=hc0):
                for i in range(nhc):
                    sync.wait_ge(eV.h, s3["gv"][i])
                mk = sST[q].inc(sync.dma_start(out=g_d[:, hc0:hc0 + nhc, base:base + T3],
                                               in_=gt[q][:, 0:nhc, :]))
                for i in range(nhc):
                    s3["g_rd"][(q, i)] = mk
            gtile += 1

    barrier()
    p3.close()
    if phases < 4:
        @block.sync
        def _(sync):
            for sem in ALL_SEMS:
                sync.wait_ge(sem.h, sem.n)
        ctx.close()
        return nc

    p4a = contextlib.ExitStack()
    sb4a = lambda n, s, d=F32: p4a.enter_context(nc.sbuf_tensor(n, s, d))
    t2A = sb4a("t2A", [P, W2A_KC, D], BF16)
    gta = [sb4a("gta0", [P, W2A_KC, T4A], BF16), sb4a("gta1", [P, W2A_KC, T4A], BF16)]
    u2s = [sb4a("u2s0", [P, T4A]), sb4a("u2s1", [P, T4A]), sb4a("u2s2", [P, T4A])]
    wst4 = [sb4a("wst40", [P, D]), sb4a("wst41", [P, D])]
    psU = [p4a.enter_context(nc.psum_tensor(f"psU{i}", [P, T4A], F32)) for i in range(3)]

    s4 = {"g": {}, "g_free": {}, "uslot": {}, "u2st": {}, "tern": {}, "strip4": {},
          "strip_g": {}, "umm": {}, "ue": {}}

    def emit_tern_w2(kc_list, dstA, tag):
        for idx, kc in enumerate(kc_list):
            q = idx % 2
            @block.gpsimd
            def _(g, kc=kc, q=q, idx=idx, tag=tag):
                prevs = s4["strip_g"].get((tag, q))
                if prevs is not None:
                    g.wait_ge(eV.h, prevs)
                sTN[q].inc(g.dma_start(out=wst4[q][:, :] if tag == "a" else wst4b[q][:, :],
                                       in_=w2h[:, kc, :]))

            @block.vector
            def _(vec, kc=kc, q=q, idx=idx, dstA=dstA, tag=tag, nj=len(kc_list)):
                vec.wait_ge(sTN[q].h, sTN[q].n)
                src = wst4[q] if tag == "a" else wst4b[q]
                vec.tensor_scalar(dstA[:, idx, :], src[:, :], scal[:, 5:6], None,
                                  op0=ALU.is_gt)
                vec.drain()
                vec.tensor_scalar(src[:, :], src[:, :], scal[:, 8:9], None, op0=ALU.is_lt)
                vec.drain()
                ins = vec.tensor_tensor(dstA[:, idx, :], dstA[:, idx, :], src[:, :],
                                        op=ALU.subtract)
                mk = eV.inc(ins)
                vec.drain()
                s4["strip4"][(tag, idx)] = mk
                s4["strip_g"][(tag, q)] = mk
                if idx == nj - 1:
                    s4["tern"][tag] = mk

    emit_tern_w2(list(range(W2A_KC)), t2A, "a")

    N4ATILES = TOWN // T4A
    for ti in range(N4ATILES):
        base = ti * T4A
        q = ti % 2

        @block.sync
        def _(sync, ti=ti, base=base, q=q):
            prev = s4["g_free"].get(q)
            if prev is not None:
                sync.wait_ge(eT.h, prev)
            sXL[q].inc(sync.dma_start(out=gta[q][:, 0:11, :],
                                      in_=g_d[:, 0:11, base:base + T4A]))
            s4["g"][ti] = sXL[q].inc(sync.dma_start(out=gta[q][:, 11:W2A_KC, :],
                                                    in_=g_d[:, 11:W2A_KC, base:base + T4A]))

        for mc in range(KC):
            slot = mc % 3
            @block.tensor
            def _(te, ti=ti, mc=mc, slot=slot, q=q):
                if mc == 0:
                    te.wait_ge(sXL[q].h, s4["g"][ti])
                    if ti == 0:
                        te.wait_ge(eV.h, s4["tern"]["a"])
                prev = s4["uslot"].get(slot)
                if prev is not None:
                    te.wait_ge(eS.h, prev)
                for kc in range(W2A_KC):
                    ins = te.matmul(psU[slot][:, :], t2A[:, kc, mc * 128:(mc + 1) * 128],
                                    gta[q][:, kc, :], start=(kc == 0), stop=(kc == W2A_KC - 1))
                mm = eT.inc(ins)
                s4["umm"][mc] = mm
                if mc == KC - 1:
                    s4["g_free"][q] = mm

            @block.scalar
            def _(sc, ti=ti, mc=mc, slot=slot, base=base):
                sc.wait_ge(eT.h, s4["umm"][mc])
                prev = s4["u2st"].get(slot)
                if prev is not None:
                    sc.wait_ge(sU[slot].h, prev)
                eS.inc(sc.activation(u2s[slot][:, :], psU[slot][:, :], AF.Copy,
                                     scale=scal[:, 2:3]))
                s4["uslot"][slot] = eS.n
                sc.wait_ge(eS.h, eS.n)
                mk = sU[slot].inc(sc.dma_start(out=u2a_d[:, mc, base:base + T4A],
                                               in_=u2s[slot][:, :]))
                s4["u2st"][slot] = mk

    barrier()
    p4a.close()
    if phases < 5:
        @block.sync
        def _(sync):
            for sem in ALL_SEMS:
                sync.wait_ge(sem.h, sem.n)
        ctx.close()
        return nc

    W2B0 = W2A_KC
    W2B_KCS = list(range(W2B0, HKC))
    NB = len(W2B_KCS)
    p4b = contextlib.ExitStack()
    sb4b = lambda n, s, d=F32: p4b.enter_context(nc.sbuf_tensor(n, s, d))
    t2B = sb4b("t2B", [P, NB, D], BF16)
    gtb = [sb4b("gtb0", [P, NB, T4B], BF16), sb4b("gtb1", [P, NB, T4B], BF16)]
    u2a_t = sb4b("u2a_t", [P, KC, T4B])
    x1bt = sb4b("x1bt", [P, KC, T4B], BF16)
    u2f = sb4b("u2f", [P, KC, T4B])
    u2bb = [sb4b(f"u2bb{i}", [P, T4B], BF16) for i in range(4)]
    sq4 = [sb4b(f"sq4{i}", [P, T4B], BF16) for i in range(4)]
    outt = sb4b("outt", [P, KC, T4B])
    scr4 = sb4b("scr4", [P, T4B])
    wst4b = [sb4b("wst4b0", [P, D]), sb4b("wst4b1", [P, D])]
    psU2 = [p4b.enter_context(nc.psum_tensor(f"psU2{i}", [P, T4B], F32)) for i in range(2)]
    psSx4 = p4b.enter_context(nc.psum_tensor("psSx4", [1, T4B], F32))
    psSxx4 = p4b.enter_context(nc.psum_tensor("psSxx4", [1, T4B], F32))
    psb4 = p4b.enter_context(nc.psum_tensor("psb4", [P, 2 * T4B], F32))

    emit_tern_w2(W2B_KCS, t2B, "b")

    sb_ = {"g": {}, "g_free": {}, "ua": {}, "ua_free": None, "uslot": {}, "u2": {},
           "u2rd": None, "u2rd_v": None, "u2b_rd": {}, "statmm": None, "bc": None,
           "bcrd": None, "outv": {}, "outst": {}, "sxrd": None, "umm": {},
           "sqb": {}, "var": None, "std": None, "muinv": None}
    N4BTILES = TOWN // T4B
    for ti in range(N4BTILES):
        base = ti * T4B
        q = ti % 2

        @block.gpsimd
        def _(g, ti=ti, base=base, q=q):
            prev = sb_["g_free"].get(q)
            if prev is not None:
                g.wait_ge(eT.h, prev)
            sG4[q].inc(g.dma_start(out=gtb[q][:, 0:11, :],
                                   in_=g_d[:, W2B0:W2B0 + 11, base:base + T4B]))
            sb_["g"][ti] = sG4[q].inc(g.dma_start(out=gtb[q][:, 11:NB, :],
                                                  in_=g_d[:, W2B0 + 11:HKC, base:base + T4B]))

        @block.sync
        def _(sync, ti=ti, base=base):
            if sb_["ua_free"] is not None:
                sync.wait_ge(eV.h, sb_["ua_free"])
            sXL[0].inc(sync.dma_start(out=u2a_t[:, :, :], in_=u2a_d[:, :, base:base + T4B]))
            sb_["ua"][ti] = sXL[0].inc(sync.dma_start(out=x1bt[:, :, :],
                                                      in_=x1_d[:, :, base:base + T4B]))

        def emit_stats4(mc, ti=ti):
            @block.tensor
            def _(te, ti=ti, mc=mc):
                kq = mc % 4
                te.wait_ge(eV.h, sb_["sqb"][mc])
                te.wait_ge(eS.h, sb_[("u2c", mc)])
                if mc == 0 and sb_["sxrd"] is not None:
                    te.wait_ge(eV.h, sb_["sxrd"])
                te.matmul(psSx4[0:1, :], onesb[:, 0:1], u2bb[kq][:, :],
                          start=(mc == 0), stop=(mc == KC - 1), skip_group_check=True)
                m2 = te.matmul(psSxx4[0:1, :], onesb[:, 0:1], sq4[kq][:, :],
                               start=(mc == 0), stop=(mc == KC - 1), skip_group_check=True)
                mm = eT.inc(m2)
                sb_["u2b_rd"][kq] = mm
                if mc == KC - 1:
                    sb_["statmm"] = mm

        for mc in range(KC):
            slot = mc % 2
            @block.tensor
            def _(te, ti=ti, mc=mc, slot=slot, q=q):
                if mc == 0:
                    te.wait_ge(sG4[q].h, sb_["g"][ti])
                    if ti == 0:
                        te.wait_ge(eV.h, s4["tern"]["b"])
                prev = sb_["uslot"].get(slot)
                if prev is not None:
                    te.wait_ge(eV.h, prev)
                for idx, kc in enumerate(W2B_KCS):
                    kd = HW_LAST if kc == HKC - 1 else 128
                    ins = te.matmul(psU2[slot][:, :], t2B[0:kd, idx, mc * 128:(mc + 1) * 128],
                                    gtb[q][0:kd, idx, :], start=(idx == 0),
                                    stop=(idx == NB - 1))
                mm = eT.inc(ins)
                sb_["umm"][mc] = mm
                if mc == KC - 1:
                    sb_["g_free"][q] = mm

            @block.vector
            def _(vec, ti=ti, mc=mc, slot=slot):
                vec.wait_ge(eT.h, sb_["umm"][mc])
                if mc == 0:
                    vec.wait_ge(sXL[0].h, sb_["ua"][ti])
                    if sb_["u2rd"] is not None:
                        vec.wait_ge(eS.h, sb_["u2rd"])
                        vec.wait_ge(eV.h, sb_["u2rd_v"])
                vec.scalar_tensor_tensor(u2f[:, mc, :], psU2[slot][:, :], scal[:, 2:3],
                                         u2a_t[:, mc, :], op0=ALU.mult, op1=ALU.add)
                vec.drain()
                ins = vec.tensor_tensor(u2f[:, mc, :], u2f[:, mc, :], x1bt[:, mc, :],
                                        op=ALU.add)
                mk = eV.inc(ins)
                sb_["uslot"][slot] = mk
                sb_["u2"][mc] = mk
                if mc == KC - 1:
                    sb_["ua_free"] = mk

            @block.scalar
            def _(sc, ti=ti, mc=mc):
                kq = mc % 4
                sc.wait_ge(eV.h, sb_["u2"][mc])
                prev = sb_["u2b_rd"].get(kq)
                if prev is not None:
                    sc.wait_ge(eT.h, prev)
                sb_[("u2c", mc)] = eS.inc(sc.copy(u2bb[kq][:, :], u2f[:, mc, :]))

            @block.vector
            def _(vec, ti=ti, mc=mc):
                kq = mc % 4
                vec.wait_ge(eV.h, sb_["u2"][mc])
                prev = sb_["u2b_rd"].get(kq)
                if prev is not None:
                    vec.wait_ge(eT.h, prev)
                ins = vec.tensor_mul(sq4[kq][:, :], u2f[:, mc, :], u2f[:, mc, :])
                sb_["sqb"][mc] = eV.inc(ins)

            if mc >= 2:
                emit_stats4(mc - 2)
        emit_stats4(14)
        emit_stats4(15)

        @block.vector
        def _(vec, ti=ti):
            vec.wait_ge(eT.h, sb_["statmm"])
            vec.tensor_scalar(muinv[0:1, 0:T4B], psSx4[0:1, :], 1.0 / D, None, op0=ALU.mult)
            vec.tensor_scalar(n2s[0:1, 0:T4B], psSxx4[0:1, :], 1.0 / D, None, op0=ALU.mult)
            vec.drain()
            sb_["sxrd"] = eV.inc(vec.nop())
            vec.tensor_tensor(inv1[0:1, 0:T4B], muinv[0:1, 0:T4B], muinv[0:1, 0:T4B],
                              op=ALU.mult)
            vec.drain()
            vec.tensor_sub(n2s[0:1, 0:T4B], n2s[0:1, 0:T4B], inv1[0:1, 0:T4B])
            vec.drain()
            ins = vec.tensor_scalar(n2s[0:1, 0:T4B], n2s[0:1, 0:T4B], LN_EPS, None,
                                    op0=ALU.add)
            sb_["var"] = eV.inc(ins)

        @block.scalar
        def _(sc, ti=ti):
            sc.wait_ge(eV.h, sb_["var"])
            sb_["std"] = eS.inc(sc.activation(n2s[0:1, 0:T4B], n2s[0:1, 0:T4B], AF.Sqrt))

        @block.vector
        def _(vec, ti=ti):
            vec.wait_ge(eS.h, sb_["std"])
            vec.reciprocal(muinv[0:1, T4B:2 * T4B], n2s[0:1, 0:T4B])
            vec.drain()
            sb_["muinv"] = eV.inc(vec.nop())

        @block.tensor
        def _(te, ti=ti):
            te.wait_ge(eV.h, sb_["muinv"])
            if sb_["bcrd"] is not None:
                te.wait_ge(eV.h, sb_["bcrd"])
            sb_["bc"] = eT.inc(te.matmul(psb4[:, :], onesf[0:1, :], muinv[0:1, 0:2 * T4B],
                                         start=True, stop=True, skip_group_check=True))

        for mc in range(KC):
            @block.vector
            def _(vec, ti=ti, mc=mc):
                vec.wait_ge(eT.h, sb_["bc"])
                if mc == 0:
                    prev = sb_["outst"].get(ti - 1)
                    if prev is not None:
                        vec.wait_ge(sOUT.h, prev)
                vec.tensor_tensor(scr4[:, :], u2f[:, mc, :], psb4[:, 0:T4B],
                                  op=ALU.subtract)
                vec.drain()
                ins = vec.tensor_tensor(outt[:, mc, :], scr4[:, :], psb4[:, T4B:2 * T4B],
                                        op=ALU.mult)
                mk = eV.inc(ins)
                vec.drain()
                sb_["outv"][mc] = mk
                if mc == KC - 1:
                    sb_["bcrd"] = mk
                    sb_["u2rd_v"] = mk
                    sb_["u2rd"] = eS.n

        @block.scalar
        def _(sc, ti=ti, base=base):
            for mc in range(KC):
                sc.wait_ge(eV.h, sb_["outv"][mc])
            sb_["outst"][ti] = sOUT.inc(sc.dma_start(out=out_d[:, :, base:base + T4B],
                                                     in_=outt[:, :, :]))

    @block.sync
    def _(sync):
        for sem in ALL_SEMS:
            sync.wait_ge(sem.h, sem.n)

    p4b.close()
    ctx.close()
    return nc


PERM = np.concatenate([np.arange(0, 1024, 2), np.arange(1, 1024, 2),
                       np.arange(1024, 2048)])


def _featmajor(a):
    r, c = a.shape
    return np.ascontiguousarray(a.reshape(r // P, P, c).transpose(1, 0, 2))


def kernel(L0_attn_X, L0_attn_W_in, L0_attn_b_in, L0_attn_W_out, L0_attn_b_out,
           L0_ffn_w1, L0_ffn_w3, L0_ffn_w2):
    X = np.asarray(L0_attn_X)
    W_in = np.asarray(L0_attn_W_in); b_in = np.asarray(L0_attn_b_in)
    W_out = np.asarray(L0_attn_W_out); b_out = np.asarray(L0_attn_b_out)
    w1 = np.asarray(L0_ffn_w1); w3 = np.asarray(L0_ffn_w3); w2 = np.asarray(L0_ffn_w2)

    WQ = W_in[0:D][PERM]; WK = W_in[D:2 * D][PERM]; WV = W_in[2 * D:3 * D][PERM]
    wqh = _featmajor(np.ascontiguousarray(WQ.T))
    wkh = _featmajor(np.ascontiguousarray(WK.T))
    wvh = _featmajor(np.ascontiguousarray(WV.T))
    woh = _featmajor(np.ascontiguousarray(W_out[:, PERM].T))
    w1h = _featmajor(np.ascontiguousarray(w1.T))
    w3h = _featmajor(np.ascontiguousarray(w3.T))
    w2T = np.zeros((HKC * P, D), np.float32)
    w2T[0:H] = w2.T
    w2h = np.ascontiguousarray(w2T.reshape(HKC, P, D).transpose(1, 0, 2))

    bq = b_in[0:D][PERM].reshape(KC, P).T
    bk = b_in[D:2 * D][PERM].reshape(KC, P).T
    bv = b_in[2 * D:3 * D][PERM].reshape(KC, P).T
    bo = b_out.reshape(KC, P).T
    bqkv = np.ascontiguousarray(np.stack([bq, bk, bv, bo], axis=1)).astype(np.float32)

    xh_all = [np.ascontiguousarray(X[b].T.reshape(KC, P, S).transpose(1, 0, 2))
              for b in range(4)]

    nc = build_program()

    in_maps = []
    for c in range(8):
        b, h = c // 2, c % 2
        s0 = h * TOWN
        in_maps.append({
            "xh": xh_all[b],
            "xho": np.ascontiguousarray(xh_all[b][:, :, s0:s0 + TOWN]),
            "wqh": wqh, "wkh": wkh, "wvh": wvh, "woh": woh,
            "w1h": w1h, "w3h": w3h, "w2h": w2h,
            "bqkv": bqkv,
            "meta": np.full((P, 1), float(s0), np.float32),
        })
    res = run_bass_kernel_spmd(nc, in_maps, core_ids=list(range(8)))

    out = np.empty((4, S, D), np.float32)
    for c in range(8):
        b, h = c // 2, c % 2
        s0 = h * TOWN
        oc = np.asarray(res.results[c]["out"])
        out[b, s0:s0 + TOWN, :] = oc.transpose(1, 0, 2).reshape(D, TOWN).T
    return out
